# version 11
# baseline (speedup 1.0000x reference)
import numpy as np
import concourse.bass as bass
import concourse.mybir as mybir
from concourse.bass_utils import run_bass_kernel_spmd

F32 = mybir.dt.float32
BF16 = mybir.dt.bfloat16
I32 = mybir.dt.int32
AF = mybir.ActivationFunctionType
ALU = mybir.AluOpType
AX = mybir.AxisListType

NCORES = 8
T = 1024
F = 4096
NI = 11008
NC = 86
FJ = 32
RT = 8
GS = 8
NG = 11
NW = 16
EPS = 1e-5


def _group_chunks(g):
    return list(range(8 * g, min(8 * g + 8, NC)))


GG_LIST = []
for _hh in range(2):
    for _g in range(NG):
        GG_LIST.append((len(GG_LIST), _hh, _g, _group_chunks(_g)))
PROD = [(gg, c) for (gg, _hh, _g, chs) in GG_LIST for c in chs]
PROD_FIRST_K = {}
PROD_LAST_K = {}
for _k, (_gg, _c) in enumerate(PROD):
    PROD_FIRST_K.setdefault(_gg, _k)
    PROD_LAST_K[_gg] = _k
CUM_CHUNKS = {}
_cum = 0
for (_gg, _hh, _g, chs) in GG_LIST:
    _cum += len(chs)
    CUM_CHUNKS[_gg] = _cum

UNITS = [(tt, fc) for fc in range(4) for tt in range(8)]
GDEQ = set(k for k in range(len(PROD)) if PROD[k][0] in (11, 12))


def cmb_idxs_at(gg):
    if gg == 10:
        return list(range(0, 16))
    return []


class MultiBuf:

    def __init__(self, ts):
        self.ts = ts

    def __getitem__(self, i):
        return self.ts[i][:]


def _multi(en, nc, kind, name, n, shape, dt):
    mk = nc.sbuf_tensor if kind == "sbuf" else nc.psum_tensor
    return MultiBuf([en(mk(f"{name}{i}", shape, dt)) for i in range(n)])


def build_nc():
    nc = bass.Bass("TRN2", target_bir_lowering=False)

    x_rows = nc.dram_tensor("x_rows", [T, F], F32, kind="ExternalInput").ap()
    qg = nc.dram_tensor("qg", [NI, F], I32, kind="ExternalInput").ap()
    qu = nc.dram_tensor("qu", [NI, F], I32, kind="ExternalInput").ap()
    wd_r = nc.dram_tensor("wd_r", [NC, 128, FJ, 128], I32, kind="ExternalInput").ap()
    sg = nc.dram_tensor("sg", [NI, 32], F32, kind="ExternalInput").ap()
    zg = nc.dram_tensor("zg", [NI, 32], F32, kind="ExternalInput").ap()
    su = nc.dram_tensor("su", [NI, 32], F32, kind="ExternalInput").ap()
    zu = nc.dram_tensor("zu", [NI, 32], F32, kind="ExternalInput").ap()
    sd_r = nc.dram_tensor("sd_r", [NC, 128, FJ], F32, kind="ExternalInput").ap()
    zd_r = nc.dram_tensor("zd_r", [NC, 128, FJ], F32, kind="ExternalInput").ap()
    ident = nc.dram_tensor("ident", [128, 128], F32, kind="ExternalInput").ap()
    out_rows = nc.dram_tensor("out_rows", [T, F], F32, kind="ExternalOutput").ap()

    h_sp = nc.dram_tensor("h_sp", [NC, 128, T], BF16).ap()
    din = nc.dram_tensor("din", [128, 96], F32)
    dout = nc.dram_tensor("dout", [128, 96], F32)
    p_dram = nc.dram_tensor("p_dram", [4, F], F32).ap()

    import contextlib
    est = contextlib.ExitStack()
    with est:
        sem = {}
        for name in ["sxA", "sxB", "ss1", "sv2", "ss2", "sxn", "sAtb", "sAev",
                     "sAd", "sdin", "scc", "spg", "s_id", "svA",
                     "s_sv", "s_sd", "s_pch", "s_sch", "s_cch",
                     "s_qA", "s_qB", "s_scA", "s_scB", "s_vd", "s_tb", "s_evA", "s_evB", "s_tuA", "s_tuB",
                     "s_mm", "s_ssil", "s_vh", "s_hwA", "s_hwB",
                     "s_hgA", "s_hgB", "s_qdA", "s_qdB", "s_sddA", "s_sddB",
                     "s_vdd", "s_tbd", "s_evdA",
                     "s_evdB", "s_mwd", "s_fld",
                     "s_pv1", "s_ps1", "s_ptr", "s_pd", "s_pb", "s_rm", "s_cab",
                     "s_xrA", "s_xrB", "s_gs", "s_cmb", "s_outA", "s_outB"]:
            sem[name] = est.enter_context(nc.semaphore(name))

        def par(base, i):
            return sem[base + ("A" if i % 2 == 0 else "B")]

        ident_bf = est.enter_context(nc.sbuf_tensor("ident_bf", [128, 128], BF16))
        ident_f = est.enter_context(nc.sbuf_tensor("ident_f", [128, 128], F32))
        m_sb = est.enter_context(nc.sbuf_tensor("m_sb", [128, 8], F32))
        r_sb = est.enter_context(nc.sbuf_tensor("r_sb", [128, 8], F32))
        dpart = est.enter_context(nc.sbuf_tensor("dpart", [128, 96], F32))
        dsum = est.enter_context(nc.sbuf_tensor("dsum", [128, 96], F32))
        p_q = est.enter_context(nc.sbuf_tensor("p_q", [128, 128], F32))
        p_s = est.enter_context(nc.sbuf_tensor("p_s", [128, 128], F32))
        p_qT = est.enter_context(nc.sbuf_tensor("p_qT", [128, 128], F32))
        negrm = est.enter_context(nc.sbuf_tensor("negrm", [128, 8], F32))

        with contextlib.ExitStack() as es1:
            en1 = es1.enter_context
            xnT = en1(nc.sbuf_tensor("xnT", [128, FJ * T], BF16))
            xbuf = en1(nc.sbuf_tensor("xbuf", [128, F], F32))
            xn_bf = en1(nc.sbuf_tensor("xn_bf", [128, F], BF16))
            sums = en1(nc.sbuf_tensor("sums", [128, 8], F32))
            sumsq = en1(nc.sbuf_tensor("sumsq", [128, 8], F32))
            msq = en1(nc.sbuf_tensor("msq", [128, 8], F32))
            tvv = en1(nc.sbuf_tensor("tvv", [128, 8], F32))
            rv = en1(nc.sbuf_tensor("rv", [128, 8], F32))
            sum01 = _multi(en1, nc, "sbuf", "sum01", 2, [128, 512], F32)
            dmp = en1(nc.sbuf_tensor("dmp", [128, 2048], BF16))
            dmp32 = en1(nc.sbuf_tensor("dmp32", [128, 512], F32))
            qg_nat = _multi(en1, nc, "sbuf", "qg_nat", 2, [128, F], BF16)
            qu_nat = _multi(en1, nc, "sbuf", "qu_nat", 2, [128, F], BF16)
            w1g_nat = en1(nc.sbuf_tensor("w1g_nat", [128, F], BF16))
            w1u_nat = en1(nc.sbuf_tensor("w1u_nat", [128, F], BF16))
            es1a = contextlib.ExitStack()
            xbuf2 = es1a.enter_context(nc.sbuf_tensor("xbuf2", [128, F], F32))
            es1a.close()
            W1Tg = _multi(en1, nc, "sbuf", "W1Tg", 2, [128, F], BF16)
            W1Tu = _multi(en1, nc, "sbuf", "W1Tu", 2, [128, F], BF16)
            sgz = _multi(en1, nc, "sbuf", "sgz", 2, [128, 4 * 8 * 32], F32)
            silu_t = en1(nc.sbuf_tensor("silu_t", [128, T], F32))
            h_buf = _multi(en1, nc, "sbuf", "h_buf", 2, [128, T], BF16)
            psg = en1(nc.psum_tensor("psg", [128, T], F32))
            psu = en1(nc.psum_tensor("psu", [128, T], F32))
            trp = _multi(en1, nc, "psum", "trp", 4, [128, 1024], BF16)
            blk = en1(nc.Block())
            xnT3 = xnT[:].rearrange("p (j t) -> p j t", t=T)

            def scale_wait_sem(c):
                blk_ = c // 8
                return par("s_sc", blk_), 64 * (blk_ // 2 + 1)

            @blk.sync
            def _(sy):
                for rt in range(RT):
                    if rt >= 2:
                        sy.wait_ge(sem["sxn"], rt - 1)
                    xb_t = xbuf if rt % 2 == 0 else xbuf2
                    sy.dma_start(out=xb_t[:], in_=x_rows[rt * 128:(rt + 1) * 128, :]
                                 ).then_inc(par("sx", rt), 16)
                for c in range(NC):
                    sy.wait_ge(sem["s_vd"], c + 1)
                    if c >= 2:
                        sy.wait_ge(sem["s_mm"], 2 * (c - 1))
                    sy.dma_start(
                        out=W1Tu[c % 2].rearrange("p (j q) -> p j q", q=128),
                        in_=w1u_nat[:].rearrange("p (j q) -> p j q", q=128),
                        transpose=True).then_inc(par("s_tu", c), 16)
                    if c >= 2:
                        sy.wait_ge(sem["s_vh"], c - 1)
                        sy.dma_start(out=h_sp[c - 2], in_=h_buf[c % 2]
                                     ).then_inc(par("s_hw", c - 2), 16)
                for c in (NC - 2, NC - 1):
                    sy.wait_ge(sem["s_vh"], c + 1)
                    sy.dma_start(out=h_sp[c], in_=h_buf[c % 2]).then_inc(par("s_hw", c), 16)

            @blk.gpsimd
            def _(g):
                g.dma_start(out=ident_bf[:], in_=ident).then_inc(sem["s_id"], 16)
                g.dma_start(out=ident_f[:], in_=ident).then_inc(sem["s_id"], 16)

                def scale_grp(blkidx):
                    n = min(8, NC - 8 * blkidx)
                    if blkidx >= 2:
                        g.wait_ge(sem["s_vd"], 8 * (blkidx - 2) + 8)
                    for ti, ten in enumerate([sg, zg, su, zu]):
                        src = ten[8 * blkidx * 128:(8 * blkidx + n) * 128, :]
                        src = src.rearrange("(c p) g -> p c g", p=128)
                        dst = sgz[blkidx % 2][:, ti * 256:ti * 256 + n * 32]
                        dst = dst.rearrange("p (c g) -> p c g", g=32)
                        g.dma_start(out=dst, in_=src).then_inc(par("s_sc", blkidx), 16)

                scale_grp(0)
                for c in range(NC):
                    if c % 8 == 0 and c > 0:
                        scale_grp(c // 8)
                    if c >= 2:
                        g.wait_ge(sem["s_vd"], c - 1)
                    g.dma_start(out=qg_nat[c % 2], in_=qg[c * 128:(c + 1) * 128, :],
                                max_dma_last_dim=8192).then_inc(par("s_q", c), 16)
                    g.dma_start(out=qu_nat[c % 2], in_=qu[c * 128:(c + 1) * 128, :],
                                max_dma_last_dim=8192).then_inc(par("s_q", c), 16)
                g.wait_ge(sem["sAd"], 1)
                g.dma_start(out=din.ap(), in_=dpart[:]).then_inc(sem["sdin"], 16)
                g.wait_ge(sem["sdin"], 16)
                g.collective_compute(
                    "AllReduce", ALU.add,
                    replica_groups=[list(range(NCORES))],
                    ins=[din.ap()], outs=[dout.ap()],
                ).then_inc(sem["scc"], 1)
                g.wait_ge(sem["scc"], 1)
                g.dma_start(out=dsum[:], in_=dout.ap()).then_inc(sem["spg"], 16)

            @blk.vector
            def _(v):
                svn = [0]

                def vstep(ins):
                    ins.then_inc(sem["svA"], 1)
                    svn[0] += 1
                    return svn[0]

                def evacA(gb):
                    rt_, b_ = gb // 4, gb % 4
                    v.wait_ge(sem["sAtb"], gb + 1)
                    dst = xnT3[:, b_ * 8:(b_ + 1) * 8, rt_ * 128:(rt_ + 1) * 128]
                    src = trp[gb % 4].rearrange("p (a q) -> p a q", q=128)
                    v.tensor_copy(dst, src).then_inc(sem["sAev"], 1)

                for rt in range(RT):
                    if rt >= 1:
                        for b in range(4):
                            evacA(4 * (rt - 1) + b)
                    xb = (xbuf if rt % 2 == 0 else xbuf2)[:]
                    cm = m_sb[:, rt:rt + 1]
                    v.wait_ge(par("sx", rt), 16 * (rt // 2 + 1))
                    k1 = vstep(v.reduce_sum(sums[:, rt:rt + 1], xb, axis=AX.X))
                    v.wait_ge(sem["ss1"], rt + 1)
                    v.wait_ge(sem["svA"], k1)
                    k2 = vstep(v.tensor_scalar(cm, sums[:, rt:rt + 1],
                                               1.0 / F, None, ALU.mult))
                    v.wait_ge(sem["svA"], k2)
                    k3 = vstep(v.tensor_scalar(msq[:, rt:rt + 1], cm, cm, EPS,
                                               ALU.mult, ALU.subtract))
                    v.wait_ge(sem["svA"], k3)
                    k4 = vstep(v.scalar_tensor_tensor(
                        tvv[:, rt:rt + 1], sumsq[:, rt:rt + 1], 1.0 / F,
                        msq[:, rt:rt + 1], ALU.mult, ALU.subtract))
                    v.wait_ge(sem["svA"], k4)
                    v.reciprocal(rv[:, rt:rt + 1], tvv[:, rt:rt + 1]
                                 ).then_inc(sem["sv2"], 1)
                    v.wait_ge(sem["ss2"], rt + 1)
                    if rt >= 1:
                        v.wait_ge(sem["sAtb"], 4 * rt)
                    v.tensor_scalar(xn_bf[:], xb, cm,
                                    r_sb[:, rt:rt + 1], ALU.subtract, ALU.mult
                                    ).then_inc(sem["sxn"], 1)
                for b in range(4):
                    evacA(4 * (RT - 1) + b)

                def deq(c):
                    v.wait_ge(par("s_q", c), 32 * (c // 2 + 1))
                    sc_sem, sc_val = scale_wait_sem(c)
                    v.wait_ge(sc_sem, sc_val)
                    if c >= 1:
                        v.wait_ge(sem["s_tb"], 4 * c)
                        v.wait_ge(par("s_tu", c - 1), 16 * ((c - 1) // 2 + 1))
                    cl = c % 8
                    for ti, (qn, wn) in enumerate([(qg_nat, w1g_nat), (qu_nat, w1u_nat)]):
                        soff = (2 * ti) * 256 + cl * 32
                        zoff = (2 * ti + 1) * 256 + cl * 32
                        for j in range(FJ):
                            ins = v.tensor_scalar(
                                wn[:, j * 128:(j + 1) * 128],
                                qn[c % 2][:, j * 128:(j + 1) * 128],
                                sgz[c // 8 % 2][:, zoff + j:zoff + j + 1],
                                sgz[c // 8 % 2][:, soff + j:soff + j + 1],
                                ALU.subtract, ALU.mult)
                    ins.then_inc(sem["s_vd"], 1)

                def evacsA(c):
                    for b in (0, 2):
                        v.wait_ge(sem["s_tb"], 4 * c + b + 1)
                        j0 = b * 8
                        dst = W1Tg[c % 2][:, j0 * 128:(j0 + 8) * 128]
                        v.tensor_copy(dst, trp[b % 4]).then_inc(sem["s_evA"], 1)

                def hTT(c):
                    v.wait_ge(sem["s_mm"], 2 * c + 2)
                    v.wait_ge(sem["s_ssil"], c + 1)
                    if c >= 2:
                        v.wait_ge(par("s_hw", c), 16 * (c // 2))
                    v.tensor_tensor(h_buf[c % 2], silu_t[:], psu[:], ALU.mult
                                    ).then_inc(sem["s_vh"], 1)

                deq(0)
                for c in range(NC):
                    evacsA(c)
                    if c + 1 < NC:
                        deq(c + 1)
                    if 2 <= c < 2 + FJ:
                        j = c - 2
                        if j >= 1:
                            v.wait_ge(sem["s_sd"], j)
                        v.tensor_tensor(sum01[j % 2], xnT3[:, j, 0:512],
                                        xnT3[:, j, 512:1024], ALU.add
                                        ).then_inc(sem["s_sv"], 1)
                    if c >= 1:
                        hTT(c - 1)
                hTT(NC - 1)

            @blk.scalar
            def _(s):
                for rt in range(RT):
                    s.wait_ge(par("sx", rt), 16 * (rt // 2 + 1))
                    if rt >= 1:
                        s.wait_ge(sem["sAtb"], 4 * rt)
                    s.activation(xn_bf[:], (xbuf if rt % 2 == 0 else xbuf2)[:], AF.Square,
                                 accum_out=sumsq[:, rt:rt + 1]).then_inc(sem["ss1"], 1)
                    s.wait_ge(sem["sv2"], rt + 1)
                    s.activation(r_sb[:, rt:rt + 1], rv[:, rt:rt + 1], AF.Sqrt
                                 ).then_inc(sem["ss2"], 1)

                def dtrio(j):
                    d0 = (j % 2) * 1024
                    s.activation(dmp[:, d0:d0 + 512], xnT3[:, j, 0:512],
                                 AF.Square, accum_out=dpart[:, 3 * j:3 * j + 1])
                    s.activation(dmp[:, d0 + 512:d0 + 1024], xnT3[:, j, 512:1024],
                                 AF.Square, accum_out=dpart[:, 3 * j + 2:3 * j + 3])
                    s.wait_ge(sem["s_sv"], j + 1)
                    ins = s.activation(dmp32[:], sum01[j % 2], AF.Square,
                                       accum_out=dpart[:, 3 * j + 1:3 * j + 2])
                    if j == FJ - 1:
                        ins.then_inc(sem["sAd"], 1)
                    else:
                        ins.then_inc(sem["s_sd"], 1)

                def evacsB(c):
                    for b in (1, 3):
                        s.wait_ge(sem["s_tb"], 4 * c + b + 1)
                        j0 = b * 8
                        dst = W1Tg[c % 2][:, j0 * 128:(j0 + 8) * 128]
                        s.copy(dst, trp[b % 4]).then_inc(sem["s_evB"], 1)

                def silu(c):
                    s.wait_ge(sem["s_mm"], 2 * c + 1)
                    if c >= 1:
                        s.wait_ge(sem["s_vh"], c)
                    s.activation(silu_t[:], psg[:], AF.Silu).then_inc(sem["s_ssil"], 1)

                for c in range(NC):
                    evacsB(c)
                    if c >= 1:
                        silu(c - 1)
                    if 2 <= c < 2 + FJ:
                        dtrio(c - 2)
                silu(NC - 1)

            @blk.tensor
            def _(t):
                t.wait_ge(sem["s_id"], 32)
                for rt in range(RT):
                    t.wait_ge(sem["sxn"], rt + 1)
                    for b in range(4):
                        gb = rt * 4 + b
                        if gb >= 4:
                            t.wait_ge(sem["sAev"], gb - 3)
                        for kk in range(8):
                            j = b * 8 + kk
                            ins = t.transpose(trp[gb % 4][:, kk * 128:(kk + 1) * 128],
                                              xn_bf[:, j * 128:(j + 1) * 128],
                                              ident_bf[:])
                        ins.then_inc(sem["sAtb"], 1)

                def tblock(c):
                    t.wait_ge(sem["s_vd"], c + 1)
                    if c == 0:
                        t.wait_ge(sem["sAev"], 32)
                    for b in range(4):
                        n = 2 * c + b // 2
                        if b % 2 == 0:
                            if n >= 2:
                                t.wait_ge(sem["s_evA"], n - 1)
                        else:
                            if n >= 2:
                                t.wait_ge(sem["s_evB"], n - 1)
                        j0 = b * 8
                        for kk in range(8):
                            ins = t.transpose(trp[b % 4][:, kk * 128:(kk + 1) * 128],
                                              w1g_nat[:, (j0 + kk) * 128:(j0 + kk + 1) * 128],
                                              ident_bf[:])
                        ins.then_inc(sem["s_tb"], 1)

                def mmblock(c):
                    t.wait_ge(sem["s_evA"], 2 * (c + 1))
                    t.wait_ge(sem["s_evB"], 2 * (c + 1))
                    t.wait_ge(par("s_tu", c), 16 * (c // 2 + 1))
                    if c >= 1:
                        t.wait_ge(sem["s_ssil"], c)
                        t.wait_ge(sem["s_vh"], c)
                    for (ps, W, k) in ((psg, W1Tg, 0), (psu, W1Tu, 1)):
                        for j in range(FJ):
                            for th in range(2):
                                ins = t.matmul(
                                    ps[:, th * 512:(th + 1) * 512],
                                    W[c % 2][:, j * 128:(j + 1) * 128],
                                    xnT3[:, j, th * 512:(th + 1) * 512],
                                    start=(j == 0), stop=(j == FJ - 1))
                        ins.then_inc(sem["s_mm"], 1)

                for c in range(NC):
                    tblock(c)
                    if c >= 1:
                        mmblock(c - 1)
                mmblock(NC - 1)

        with contextlib.ExitStack() as es2:
            en2 = es2.enter_context
            h_grp = _multi(en2, nc, "sbuf", "h_grp", 2, [128, GS * T], BF16)
            qd_buf = _multi(en2, nc, "sbuf", "qd_buf", 2, [128, 2048], BF16)
            w1d_nat = en2(nc.sbuf_tensor("w1d_nat", [128, 2048], BF16))
            W1dT = _multi(en2, nc, "sbuf", "W1dT", 2, [128, GS * 2048], BF16)
            sdz = _multi(en2, nc, "sbuf", "sdz", 2, [128, 2 * 8 * 32], F32)
            dacc = en2(nc.sbuf_tensor("dacc", [128, 8 * 2048], F32))
            pb4 = _multi(en2, nc, "sbuf", "pb4", 2, [128, 2048], BF16)
            xs_buf = _multi(en2, nc, "sbuf", "xs_buf", 2, [128, 512], F32)
            xp_buf = _multi(en2, nc, "sbuf", "xp_buf", 2, [128, 512], F32)
            g_sb = _multi(en2, nc, "sbuf", "g_sb", 2, [128, 512], F32)
            cA = en2(nc.sbuf_tensor("cA", [128, 512], F32))
            cB = en2(nc.sbuf_tensor("cB", [128, 512], F32))
            cC = en2(nc.psum_tensor("cC", [128, 512], F32))
            out_sb = _multi(en2, nc, "sbuf", "out_sb", 2, [128, 512], F32)
            punit = _multi(en2, nc, "psum", "punit", 4, [128, 512], F32)
            trp2 = _multi(en2, nc, "psum", "trp2", 2, [128, 1024], BF16)
            pps = en2(nc.psum_tensor("pps", [128, 128], F32))
            blk2 = en2(nc.Block())
            NSP_A = 16 * ((NC + 1) // 2)
            NSP_B = 16 * (NC // 2)

            def hw_barrier(eng):
                eng.wait_ge(sem["s_hwA"], NSP_A)
                eng.wait_ge(sem["s_hwB"], NSP_B)

            K5_OF = {}

            def xr_idx(idx):
                hh2, fc, tt = idx // 32, (idx % 32) // 8, idx % 8
                pt = (tt + 4) % 8
                col = hh2 * 2048 + fc * 512
                return hh2, fc, tt, pt, col

            @blk2.sync
            def _(sy):
                hw_barrier(sy)
                for (gg, hh, g, chs) in GG_LIST:
                    if gg >= 2:
                        sy.wait_ge(sem["s_mwd"], NW * (gg - 1))
                    src = h_sp[chs[0]:chs[0] + len(chs)].rearrange("c p t -> p c t")
                    dst = h_grp[gg % 2][:, 0:len(chs) * T]
                    dst = dst.rearrange("p (c t) -> p c t", t=T)
                    sy.dma_start(out=dst, in_=src).then_inc(par("s_hg", gg), 16)

            @blk2.gpsimd
            def _(g):
                hw_barrier(g)

                def sdz_grp(gg):
                    hh, gl = gg // NG, gg % NG
                    chs = _group_chunks(gl)
                    if gg >= 2:
                        g.wait_ge(sem["s_vdd"], PROD_LAST_K[gg - 2] + 1)
                    for ti, ten in enumerate([sd_r, zd_r]):
                        src = ten[chs[0]:chs[0] + len(chs)].rearrange("c p g -> p c g")
                        dst = sdz[gg % 2][:, ti * 256:ti * 256 + len(chs) * 32]
                        dst = dst.rearrange("p (c g) -> p c g", g=32)
                        g.dma_start(out=dst, in_=src).then_inc(par("s_sdd", gg), 16)

                def qd_dma(k):
                    gg, c = PROD[k]
                    hh = gg // NG
                    if k >= 2:
                        g.wait_ge(sem["s_vdd"], k - 1)
                    src = wd_r[c, :, 16 * hh:16 * hh + 16, :]
                    dst = qd_buf[k % 2].rearrange("p (a b) -> p a b", b=128)
                    g.dma_start(out=dst, in_=src,
                                max_dma_last_dim=8192).then_inc(par("s_qd", k), 16)

                def deq_d_g(k):
                    gg2, c2 = PROD[k]
                    hh2 = gg2 // NG
                    g.wait_ge(par("s_qd", k), 16 * (k // 2 + 1))
                    g.wait_ge(par("s_sdd", gg2), 32 * (gg2 // 2 + 1))
                    g.wait_ge(sem["s_tbd"], 2 * k)
                    cl2 = c2 % 8
                    for j in range(16):
                        jj = 16 * hh2 + j
                        ins = g.tensor_scalar(
                            w1d_nat[:, j * 128:(j + 1) * 128],
                            qd_buf[k % 2][:, j * 128:(j + 1) * 128],
                            sdz[gg2 % 2][:, 256 + cl2 * 32 + jj:256 + cl2 * 32 + jj + 1],
                            sdz[gg2 % 2][:, cl2 * 32 + jj:cl2 * 32 + jj + 1],
                            ALU.subtract, ALU.mult)
                    ins.then_inc(sem["s_vdd"], 1)

                sdz_grp(0)
                for k in range(PROD_FIRST_K[0], PROD_LAST_K[0] + 1):
                    qd_dma(k)
                for (gg, hh, gl, chs) in GG_LIST:
                    if gg + 1 < len(GG_LIST):
                        sdz_grp(gg + 1)
                        kf = PROD_FIRST_K[gg + 1]
                        kl = PROD_LAST_K[gg + 1]
                        if gg + 1 in (11, 12):
                            for k in range(kf, kl + 1):
                                if k - 2 >= kf:
                                    deq_d_g(k - 2)
                                qd_dma(k)
                            deq_d_g(kl - 1)
                            deq_d_g(kl)
                        else:
                            for k in range(kf, kl + 1):
                                qd_dma(k)
                    if gg == 10 or gg == 21:
                        hh2 = 0 if gg == 10 else 1
                        if hh2 == 0:
                            g.wait_ge(sem["s_pd"], 16)
                        else:
                            g.wait_ge(sem["s_cmb"], 32)
                        for bi, ch in enumerate((0, 2)):
                            src = p_dram[ch:ch + 1, hh2 * 2048:(hh2 + 1) * 2048]
                            src = src.partition_broadcast(128)[:, 0, :]
                            g.dma_start(out=pb4[bi], in_=src,
                                        max_dma_last_dim=8192
                                        ).then_inc(sem["s_pb"], 16)
                g.wait_ge(sem["s_outA"], 16 * 32)
                g.wait_ge(sem["s_outB"], 16 * 32)

            @blk2.vector
            def _(v):
                hw_barrier(v)
                v.tensor_tensor(negrm[:], m_sb[:], r_sb[:], ALU.mult
                                ).then_inc(sem["s_cch"], 1)
                ccn0_rm = 1
                v.wait_ge(sem["s_cch"], ccn0_rm)
                v.tensor_scalar(negrm[:], negrm[:], -1.0, None, ALU.mult
                                ).then_inc(sem["s_rm"], 1)

                def p_chain():
                    v.wait_ge(sem["spg"], 16)
                    dA = dsum[:, 0:96:3]
                    dP = dsum[:, 1:96:3]
                    dB = dsum[:, 2:96:3]
                    v.tensor_tensor(p_q[:, 0:32], dA, dB, ALU.add
                                    ).then_inc(sem["s_pch"], 1)
                    v.wait_ge(sem["s_pch"], 1)
                    v.tensor_tensor(p_q[:, 32:64], p_q[:, 0:32], dP, ALU.subtract
                                    ).then_inc(sem["s_pch"], 1)
                    v.wait_ge(sem["s_pch"], 2)
                    v.scalar_tensor_tensor(p_q[:, 64:96], p_q[:, 32:64], 0.5, dA,
                                           ALU.mult, ALU.add).then_inc(sem["s_pch"], 1)
                    v.scalar_tensor_tensor(p_q[:, 96:128], p_q[:, 32:64], 0.5, dB,
                                           ALU.mult, ALU.add).then_inc(sem["s_pv1"], 1)

                def deq_d(k):
                    gg, c = PROD[k]
                    hh = gg // NG
                    v.wait_ge(par("s_qd", k), 16 * (k // 2 + 1))
                    v.wait_ge(par("s_sdd", gg), 32 * (gg // 2 + 1))
                    if k >= 1:
                        v.wait_ge(sem["s_tbd"], 2 * k)
                    cl = c % 8
                    for j in range(16):
                        jj = 16 * hh + j
                        ins = v.tensor_scalar(
                            w1d_nat[:, j * 128:(j + 1) * 128],
                            qd_buf[k % 2][:, j * 128:(j + 1) * 128],
                            sdz[gg % 2][:, 256 + cl * 32 + jj:256 + cl * 32 + jj + 1],
                            sdz[gg % 2][:, cl * 32 + jj:cl * 32 + jj + 1],
                            ALU.subtract, ALU.mult)
                    ins.then_inc(sem["s_vdd"], 1)

                def flush(gg, w):
                    W = NW * gg + w
                    v.wait_ge(sem["s_mwd"], W + 1)
                    first = (gg % NG == 0)
                    if first and gg >= NG:
                        v.wait_ge(sem["s_gs"], 2 * w + 2)
                    for u in range(2):
                        tt, fc = UNITS[2 * w + u]
                        dst = dacc[:, tt * 2048 + fc * 512:tt * 2048 + fc * 512 + 512]
                        if first:
                            ins = v.tensor_copy(dst, punit[(w % 2) * 2 + u])
                        else:
                            ins = v.tensor_tensor(dst, dst, punit[(w % 2) * 2 + u],
                                                  ALU.add)
                    ins.then_inc(sem["s_fld"], 1)

                ccn = [1]

                def cstep(ins):
                    ins.then_inc(sem["s_cch"], 1)
                    ccn[0] += 1
                    return ccn[0]

                def cmb(idx):
                    hh, fc, tt, pt, col = xr_idx(idx)
                    v.wait_ge(sem["s_pb"], 32 * (hh + 1))
                    v.wait_ge(par("s_xr", idx), 32 * (idx // 2 + 1))
                    v.wait_ge(sem["s_gs"], idx + 1)
                    v.wait_ge(sem["s_cab"], 2 * (idx + 1))
                    if idx >= 2:
                        v.wait_ge(par("s_out", idx), 16 * (idx // 2))
                    fs = slice(fc * 512, (fc + 1) * 512)
                    pbX = pb4[0][:, fs] if tt < 4 else pb4[1][:, fs]
                    first, second = (cA, cB) if tt < 4 else (cB, cA)
                    k3 = cstep(v.tensor_tensor(cC[:], first[:], second[:],
                                               ALU.subtract))
                    v.wait_ge(sem["s_cch"], k3)
                    k4 = cstep(v.tensor_tensor(first[:], cC[:], pbX, ALU.mult))
                    v.wait_ge(sem["s_cch"], k4)
                    k5 = cstep(v.scalar_tensor_tensor(cC[:], second[:], 2.0,
                                                      first[:], ALU.mult, ALU.add))
                    K5_OF[idx] = k5
                    v.wait_ge(sem["s_cch"], k5)
                    k6 = cstep(v.tensor_tensor(first[:], cC[:], g_sb[idx % 2],
                                               ALU.mult))
                    v.wait_ge(sem["s_cch"], k6)
                    v.tensor_tensor(out_sb[idx % 2], first[:], xs_buf[idx % 2],
                                    ALU.add).then_inc(sem["s_cmb"], 1)

                for k in range(PROD_FIRST_K[0], PROD_LAST_K[0] + 1):
                    deq_d(k)
                for (gg, hh, gl, chs) in GG_LIST:
                    nxt = (list(range(PROD_FIRST_K[gg + 1], PROD_LAST_K[gg + 1] + 1))
                           if gg + 1 < len(GG_LIST) else [])
                    for w in range(NW):
                        if w < len(nxt) and nxt[w] not in GDEQ:
                            deq_d(nxt[w])
                        if gg == NG and w >= 8:
                            cmb(2 * w)
                            cmb(2 * w + 1)
                        flush(gg, w)
                        if gg == 21 and w >= 1:
                            cmb(32 + 2 * (w - 1))
                            cmb(32 + 2 * (w - 1) + 1)
                    for k in nxt[NW:]:
                        if k not in GDEQ:
                            deq_d(k)
                    if gg == 7:
                        p_chain()
                    for idx in cmb_idxs_at(gg):
                        cmb(idx)
                for idx in range(62, 64):
                    cmb(idx)

            @blk2.scalar
            def _(s):
                hw_barrier(s)

                def p_sig():
                    s.wait_ge(sem["s_pv1"], 1)
                    s.activation(p_s[:, 0:32], p_q[:, 64:96],
                                 AF.Sigmoid, scale=1.0 / 64)
                    s.activation(p_s[:, 32:64], p_q[:, 64:96],
                                 AF.Sigmoid, scale=-1.0 / 64)
                    s.activation(p_s[:, 64:96], p_q[:, 96:128],
                                 AF.Sigmoid, scale=-1.0 / 64)
                    s.activation(p_s[:, 96:128], p_q[:, 96:128],
                                 AF.Sigmoid, scale=1.0 / 64
                                 ).then_inc(sem["s_sch"], 1)
                    s.wait_ge(sem["s_sch"], 1)
                    s.add(p_s[:, 0:32], p_s[:, 0:32], 1.0)
                    s.add(p_s[:, 96:128], p_s[:, 96:128], 1.0
                          ).then_inc(sem["s_ps1"], 1)

                def p_store():
                    s.wait_ge(sem["s_ptr"], 1)
                    s.copy(p_qT[:], pps[:]).then_inc(sem["s_sch"], 1)
                    s.wait_ge(sem["s_sch"], 2)
                    pd = p_dram.rearrange("c (j q) -> (c j) q", q=128)
                    s.dma_start(out=pd, in_=p_qT[:]).then_inc(sem["s_pd"], 16)

                def evac_dA(k):
                    gg, c = PROD[k]
                    slot = c % 8
                    s.wait_ge(sem["s_tbd"], 2 * k + 1)
                    if gg >= 2:
                        s.wait_ge(sem["s_mwd"], NW * (gg - 1))
                    dst = W1dT[gg % 2][:, slot * 2048:slot * 2048 + 1024]
                    s.copy(dst, trp2[0]).then_inc(sem["s_evdA"], 1)

                def evac_dB(k):
                    gg, c = PROD[k]
                    slot = c % 8
                    s.wait_ge(sem["s_tbd"], 2 * k + 2)
                    if gg >= 2:
                        s.wait_ge(sem["s_mwd"], NW * (gg - 1))
                    dst = W1dT[gg % 2][:, slot * 2048 + 1024:slot * 2048 + 2048]
                    s.copy(dst, trp2[1]).then_inc(sem["s_evdB"], 1)

                def xr_pair(idx):
                    hh2, fc, tt, pt, col = xr_idx(idx)
                    s.dma_start(
                        out=xs_buf[idx % 2],
                        in_=x_rows[tt * 128:(tt + 1) * 128, col:col + 512]
                    ).then_inc(par("s_xr", idx), 16)
                    s.dma_start(
                        out=xp_buf[idx % 2],
                        in_=x_rows[pt * 128:(pt + 1) * 128, col:col + 512]
                    ).then_inc(par("s_xr", idx), 16)

                def sg_out(idx):
                    hh, fc, tt, pt, col = xr_idx(idx)
                    ui = fc * 8 + tt
                    gg_last = NG - 1 + hh * NG
                    s.wait_ge(sem["s_fld"], NW * gg_last + ui // 2 + 1)
                    if idx >= 2:
                        s.wait_ge(sem["s_cmb"], idx - 1)
                    src = dacc[:, tt * 2048 + fc * 512:tt * 2048 + fc * 512 + 512]
                    s.activation(g_sb[idx % 2], src, AF.Silu
                                 ).then_inc(sem["s_gs"], 1)
                    if idx == 0:
                        s.wait_ge(sem["s_rm"], 1)
                    s.wait_ge(par("s_xr", idx), 32 * (idx // 2 + 1))
                    if idx >= 1:
                        s.wait_ge(sem["s_cch"], K5_OF[idx - 1])
                    s.activation(cA[:], xs_buf[idx % 2], AF.Identity,
                                 scale=r_sb[:, tt:tt + 1],
                                 bias=negrm[:, tt:tt + 1]).then_inc(sem["s_cab"], 1)
                    s.activation(cB[:], xp_buf[idx % 2], AF.Identity,
                                 scale=r_sb[:, pt:pt + 1],
                                 bias=negrm[:, pt:pt + 1]).then_inc(sem["s_cab"], 1)
                    s.wait_ge(sem["s_cmb"], idx + 1)
                    s.dma_start(out=out_rows[tt * 128:(tt + 1) * 128,
                                             col:col + 512],
                                in_=out_sb[idx % 2]).then_inc(par("s_out", idx), 16)
                    if idx + 2 < 64:
                        xr_pair(idx + 2)

                for k in range(PROD_FIRST_K[0], PROD_LAST_K[0] + 1):
                    evac_dA(k)
                    evac_dB(k)
                for (gg, hh, gl, chs) in GG_LIST:
                    if gg + 1 < len(GG_LIST):
                        for k in range(PROD_FIRST_K[gg + 1], PROD_LAST_K[gg + 1] + 1):
                            evac_dA(k)
                            evac_dB(k)
                    if gg == 8:
                        p_sig()
                    if gg == 9:
                        p_store()
                        xr_pair(0)
                        xr_pair(1)
                    for idx in cmb_idxs_at(gg):
                        sg_out(idx)
                    if gg == NG:
                        for idx in range(16, 32):
                            sg_out(idx)
                for idx in range(32, 64):
                    sg_out(idx)

            @blk2.tensor
            def _(t):
                hw_barrier(t)

                def tb_d(k):
                    gg, c = PROD[k]
                    t.wait_ge(sem["s_vdd"], k + 1)
                    for b in range(2):
                        if k >= 1:
                            t.wait_ge(sem["s_evdA" if b == 0 else "s_evdB"], k)
                        for kk in range(8):
                            j = b * 8 + kk
                            ins = t.transpose(trp2[b][:, kk * 128:(kk + 1) * 128],
                                              w1d_nat[:, j * 128:(j + 1) * 128],
                                              ident_bf[:])
                        ins.then_inc(sem["s_tbd"], 1)

                def wave(gg, gl, chs, w):
                    W = NW * gg + w
                    if W >= 2:
                        t.wait_ge(sem["s_fld"], W - 1)
                    if w == 0:
                        t.wait_ge(par("s_hg", gg), 16 * (gg // 2 + 1))
                    k0 = PROD_FIRST_K[gg]
                    for ci in range(len(chs)):
                        if w == 0:
                            t.wait_ge(sem["s_evdA"], k0 + ci + 1)
                            t.wait_ge(sem["s_evdB"], k0 + ci + 1)
                        for u in range(2):
                            tt, fc = UNITS[2 * w + u]
                            ins = t.matmul(
                                punit[(w % 2) * 2 + u],
                                h_grp[gg % 2][:, ci * T + tt * 128:ci * T + tt * 128 + 128],
                                W1dT[gg % 2][:, ci * 2048 + fc * 512:ci * 2048 + fc * 512 + 512],
                                start=(ci == 0), stop=(ci == len(chs) - 1))
                    ins.then_inc(sem["s_mwd"], 1)

                for k in range(PROD_FIRST_K[0], PROD_LAST_K[0] + 1):
                    tb_d(k)
                for (gg, hh, gl, chs) in GG_LIST:
                    nxt = (list(range(PROD_FIRST_K[gg + 1], PROD_LAST_K[gg + 1] + 1))
                           if gg + 1 < len(GG_LIST) else [])
                    for w in range(NW):
                        wave(gg, gl, chs, w)
                        if w < len(nxt):
                            tb_d(nxt[w])
                    for k in nxt[NW:]:
                        tb_d(k)
                    if gg == 8:
                        t.wait_ge(sem["s_ps1"], 1)
                        t.transpose(pps[:], p_s[:], ident_f[:]).then_inc(sem["s_ptr"], 1)

    return nc


def kernel(L0_attn_x, L0_ffn_gate_qweight, L0_ffn_gate_scales, L0_ffn_gate_zeros,
           L0_ffn_up_qweight, L0_ffn_up_scales, L0_ffn_up_zeros,
           L0_ffn_down_qweight, L0_ffn_down_scales, L0_ffn_down_zeros):
    x = np.ascontiguousarray(np.asarray(L0_attn_x, dtype=np.float32))
    qg = np.ascontiguousarray(np.asarray(L0_ffn_gate_qweight, dtype=np.int32))
    qu = np.ascontiguousarray(np.asarray(L0_ffn_up_qweight, dtype=np.int32))
    qd = np.asarray(L0_ffn_down_qweight, dtype=np.int32)
    wd_r = np.ascontiguousarray(
        qd.reshape(FJ, 128, NC, 128).transpose(2, 1, 0, 3))
    sd = np.asarray(L0_ffn_down_scales, dtype=np.float32)
    zd = np.asarray(L0_ffn_down_zeros, dtype=np.float32)
    sd_r = np.ascontiguousarray(sd.reshape(FJ, 128, NC).transpose(2, 1, 0))
    zd_r = np.ascontiguousarray(zd.reshape(FJ, 128, NC).transpose(2, 1, 0))
    shared = {
        "qg": qg, "qu": qu, "wd_r": wd_r,
        "sg": np.ascontiguousarray(np.asarray(L0_ffn_gate_scales, np.float32)),
        "zg": np.ascontiguousarray(np.asarray(L0_ffn_gate_zeros, np.float32)),
        "su": np.ascontiguousarray(np.asarray(L0_ffn_up_scales, np.float32)),
        "zu": np.ascontiguousarray(np.asarray(L0_ffn_up_zeros, np.float32)),
        "sd_r": sd_r, "zd_r": zd_r,
        "ident": np.eye(128, dtype=np.float32),
    }
    nc = build_nc()
    in_maps = []
    for k in range(NCORES):
        m = dict(shared)
        m["x_rows"] = np.ascontiguousarray(
            x[:, k * 512:(k + 1) * 512, :]).reshape(T, F)
        in_maps.append(m)
    res = run_bass_kernel_spmd(nc, in_maps, core_ids=list(range(NCORES)))
    out = np.empty((2, 4096, 4096), np.float32)
    for k in range(NCORES):
        out[:, k * 512:(k + 1) * 512, :] = res.results[k]["out_rows"].reshape(2, 512, F)
    return out


# version 12
# speedup vs baseline: 1.0700x; 1.0700x over previous
import numpy as np
import concourse.bass as bass
import concourse.mybir as mybir
from concourse.bass_utils import run_bass_kernel_spmd

F32 = mybir.dt.float32
BF16 = mybir.dt.bfloat16
I32 = mybir.dt.int32
AF = mybir.ActivationFunctionType
ALU = mybir.AluOpType
AX = mybir.AxisListType

NCORES = 8
T = 1024
F = 4096
NI = 11008
NC = 86
FJ = 32
RT = 8
GS = 8
NG = 11
NW = 16
EPS = 1e-5


def _group_chunks(g):
    return list(range(8 * g, min(8 * g + 8, NC)))


GG_LIST = []
for _hh in range(2):
    for _g in range(NG):
        GG_LIST.append((len(GG_LIST), _hh, _g, _group_chunks(_g)))
PROD = [(gg, c) for (gg, _hh, _g, chs) in GG_LIST for c in chs]
PROD_FIRST_K = {}
PROD_LAST_K = {}
for _k, (_gg, _c) in enumerate(PROD):
    PROD_FIRST_K.setdefault(_gg, _k)
    PROD_LAST_K[_gg] = _k
CUM_CHUNKS = {}
_cum = 0
for (_gg, _hh, _g, chs) in GG_LIST:
    _cum += len(chs)
    CUM_CHUNKS[_gg] = _cum

UNITS = [(tt, fc) for fc in range(4) for tt in range(8)]
GDEQ = set()


def cmb_idxs_at(gg):
    if gg == 10:
        return list(range(0, 16))
    return []


class MultiBuf:

    def __init__(self, ts):
        self.ts = ts

    def __getitem__(self, i):
        return self.ts[i][:]


def _multi(en, nc, kind, name, n, shape, dt):
    mk = nc.sbuf_tensor if kind == "sbuf" else nc.psum_tensor
    return MultiBuf([en(mk(f"{name}{i}", shape, dt)) for i in range(n)])


def build_nc():
    nc = bass.Bass("TRN2", target_bir_lowering=False)

    x_rows = nc.dram_tensor("x_rows", [T, F], F32, kind="ExternalInput").ap()
    qg = nc.dram_tensor("qg", [NI, F], I32, kind="ExternalInput").ap()
    qu = nc.dram_tensor("qu", [NI, F], I32, kind="ExternalInput").ap()
    wd_r = nc.dram_tensor("wd_r", [NC, 128, FJ, 128], I32, kind="ExternalInput").ap()
    sg = nc.dram_tensor("sg", [NI, 32], F32, kind="ExternalInput").ap()
    zg = nc.dram_tensor("zg", [NI, 32], F32, kind="ExternalInput").ap()
    su = nc.dram_tensor("su", [NI, 32], F32, kind="ExternalInput").ap()
    zu = nc.dram_tensor("zu", [NI, 32], F32, kind="ExternalInput").ap()
    sd_r = nc.dram_tensor("sd_r", [NC, 128, FJ], F32, kind="ExternalInput").ap()
    zd_r = nc.dram_tensor("zd_r", [NC, 128, FJ], F32, kind="ExternalInput").ap()
    ident = nc.dram_tensor("ident", [128, 128], F32, kind="ExternalInput").ap()
    out_rows = nc.dram_tensor("out_rows", [T, F], F32, kind="ExternalOutput").ap()

    h_sp = nc.dram_tensor("h_sp", [NC, 128, T], BF16).ap()
    din = nc.dram_tensor("din", [128, 96], F32)
    dout = nc.dram_tensor("dout", [128, 96], F32)
    p_dram = nc.dram_tensor("p_dram", [4, F], F32).ap()

    import contextlib
    est = contextlib.ExitStack()
    with est:
        sem = {}
        for name in ["sxA", "sxB", "ss1", "sv2", "ss2", "sxn", "sAtb", "sAev",
                     "sAd", "sdin", "scc", "spg", "s_id", "svA",
                     "s_sv", "s_sd", "s_pch", "s_sch", "s_cch",
                     "s_qA", "s_qB", "s_scA", "s_scB", "s_vd", "s_tb", "s_evA", "s_evB", "s_tuA", "s_tuB",
                     "s_mm", "s_ssil", "s_vh", "s_hwA", "s_hwB",
                     "s_hgA", "s_hgB", "s_qdA", "s_qdB", "s_sddA", "s_sddB",
                     "s_vdd", "s_tbd", "s_evdA",
                     "s_evdB", "s_mwd", "s_fld",
                     "s_pv1", "s_ps1", "s_ptr", "s_pd", "s_pb", "s_rm", "s_cab",
                     "s_xrA", "s_xrB", "s_gs", "s_cmb", "s_outA", "s_outB"]:
            sem[name] = est.enter_context(nc.semaphore(name))

        def par(base, i):
            return sem[base + ("A" if i % 2 == 0 else "B")]

        ident_bf = est.enter_context(nc.sbuf_tensor("ident_bf", [128, 128], BF16))
        ident_f = est.enter_context(nc.sbuf_tensor("ident_f", [128, 128], F32))
        m_sb = est.enter_context(nc.sbuf_tensor("m_sb", [128, 8], F32))
        r_sb = est.enter_context(nc.sbuf_tensor("r_sb", [128, 8], F32))
        dpart = est.enter_context(nc.sbuf_tensor("dpart", [128, 96], F32))
        dsum = est.enter_context(nc.sbuf_tensor("dsum", [128, 96], F32))
        p_q = est.enter_context(nc.sbuf_tensor("p_q", [128, 128], F32))
        p_s = est.enter_context(nc.sbuf_tensor("p_s", [128, 128], F32))
        p_qT = est.enter_context(nc.sbuf_tensor("p_qT", [128, 128], F32))
        negrm = est.enter_context(nc.sbuf_tensor("negrm", [128, 8], F32))

        with contextlib.ExitStack() as es1:
            en1 = es1.enter_context
            xnT = en1(nc.sbuf_tensor("xnT", [128, FJ * T], BF16))
            xbuf = en1(nc.sbuf_tensor("xbuf", [128, F], F32))
            xn_bf = en1(nc.sbuf_tensor("xn_bf", [128, F], BF16))
            sums = en1(nc.sbuf_tensor("sums", [128, 8], F32))
            sumsq = en1(nc.sbuf_tensor("sumsq", [128, 8], F32))
            msq = en1(nc.sbuf_tensor("msq", [128, 8], F32))
            tvv = en1(nc.sbuf_tensor("tvv", [128, 8], F32))
            rv = en1(nc.sbuf_tensor("rv", [128, 8], F32))
            sum01 = _multi(en1, nc, "sbuf", "sum01", 2, [128, 512], F32)
            dmp = en1(nc.sbuf_tensor("dmp", [128, 2048], BF16))
            dmp32 = en1(nc.sbuf_tensor("dmp32", [128, 512], F32))
            qg_nat = _multi(en1, nc, "sbuf", "qg_nat", 2, [128, F], BF16)
            qu_nat = _multi(en1, nc, "sbuf", "qu_nat", 2, [128, F], BF16)
            w1g_nat = en1(nc.sbuf_tensor("w1g_nat", [128, F], BF16))
            w1u_nat = en1(nc.sbuf_tensor("w1u_nat", [128, F], BF16))
            es1a = contextlib.ExitStack()
            xbuf2 = es1a.enter_context(nc.sbuf_tensor("xbuf2", [128, F], F32))
            es1a.close()
            W1Tg = _multi(en1, nc, "sbuf", "W1Tg", 2, [128, F], BF16)
            W1Tu = _multi(en1, nc, "sbuf", "W1Tu", 2, [128, F], BF16)
            sgz = _multi(en1, nc, "sbuf", "sgz", 2, [128, 4 * 8 * 32], F32)
            silu_t = en1(nc.sbuf_tensor("silu_t", [128, T], F32))
            h_buf = _multi(en1, nc, "sbuf", "h_buf", 2, [128, T], BF16)
            psg = en1(nc.psum_tensor("psg", [128, T], F32))
            psu = en1(nc.psum_tensor("psu", [128, T], F32))
            trp = _multi(en1, nc, "psum", "trp", 4, [128, 1024], BF16)
            blk = en1(nc.Block())
            xnT3 = xnT[:].rearrange("p (j t) -> p j t", t=T)

            def scale_wait_sem(c):
                blk_ = c // 8
                return par("s_sc", blk_), 64 * (blk_ // 2 + 1)

            @blk.sync
            def _(sy):
                for rt in range(RT):
                    if rt >= 2:
                        sy.wait_ge(sem["sxn"], rt - 1)
                    xb_t = xbuf if rt % 2 == 0 else xbuf2
                    sy.dma_start(out=xb_t[:], in_=x_rows[rt * 128:(rt + 1) * 128, :]
                                 ).then_inc(par("sx", rt), 16)
                for c in range(NC):
                    sy.wait_ge(sem["s_vd"], c + 1)
                    if c >= 2:
                        sy.wait_ge(sem["s_mm"], 2 * (c - 1))
                    sy.dma_start(
                        out=W1Tu[c % 2].rearrange("p (j q) -> p j q", q=128),
                        in_=w1u_nat[:].rearrange("p (j q) -> p j q", q=128),
                        transpose=True).then_inc(par("s_tu", c), 16)
                    if c >= 2:
                        sy.wait_ge(sem["s_vh"], c - 1)
                        sy.dma_start(out=h_sp[c - 2], in_=h_buf[c % 2]
                                     ).then_inc(par("s_hw", c - 2), 16)
                for c in (NC - 2, NC - 1):
                    sy.wait_ge(sem["s_vh"], c + 1)
                    sy.dma_start(out=h_sp[c], in_=h_buf[c % 2]).then_inc(par("s_hw", c), 16)

            @blk.gpsimd
            def _(g):
                g.dma_start(out=ident_bf[:], in_=ident).then_inc(sem["s_id"], 16)
                g.dma_start(out=ident_f[:], in_=ident).then_inc(sem["s_id"], 16)

                def scale_grp(blkidx):
                    n = min(8, NC - 8 * blkidx)
                    if blkidx >= 2:
                        g.wait_ge(sem["s_vd"], 8 * (blkidx - 2) + 8)
                    for ti, ten in enumerate([sg, zg, su, zu]):
                        src = ten[8 * blkidx * 128:(8 * blkidx + n) * 128, :]
                        src = src.rearrange("(c p) g -> p c g", p=128)
                        dst = sgz[blkidx % 2][:, ti * 256:ti * 256 + n * 32]
                        dst = dst.rearrange("p (c g) -> p c g", g=32)
                        g.dma_start(out=dst, in_=src).then_inc(par("s_sc", blkidx), 16)

                scale_grp(0)
                for c in range(NC):
                    if c % 8 == 0 and c > 0:
                        scale_grp(c // 8)
                    if c >= 2:
                        g.wait_ge(sem["s_vd"], c - 1)
                    g.dma_start(out=qg_nat[c % 2], in_=qg[c * 128:(c + 1) * 128, :],
                                max_dma_last_dim=8192).then_inc(par("s_q", c), 16)
                    g.dma_start(out=qu_nat[c % 2], in_=qu[c * 128:(c + 1) * 128, :],
                                max_dma_last_dim=8192).then_inc(par("s_q", c), 16)
                g.wait_ge(sem["sAd"], 1)
                g.dma_start(out=din.ap(), in_=dpart[:]).then_inc(sem["sdin"], 16)
                g.wait_ge(sem["sdin"], 16)
                g.collective_compute(
                    "AllReduce", ALU.add,
                    replica_groups=[list(range(NCORES))],
                    ins=[din.ap()], outs=[dout.ap()],
                ).then_inc(sem["scc"], 1)
                g.wait_ge(sem["scc"], 1)
                g.dma_start(out=dsum[:], in_=dout.ap()).then_inc(sem["spg"], 16)

            @blk.vector
            def _(v):
                svn = [0]

                def vstep(ins):
                    ins.then_inc(sem["svA"], 1)
                    svn[0] += 1
                    return svn[0]

                def evacA(gb):
                    rt_, b_ = gb // 4, gb % 4
                    v.wait_ge(sem["sAtb"], gb + 1)
                    dst = xnT3[:, b_ * 8:(b_ + 1) * 8, rt_ * 128:(rt_ + 1) * 128]
                    src = trp[gb % 4].rearrange("p (a q) -> p a q", q=128)
                    v.tensor_copy(dst, src).then_inc(sem["sAev"], 1)

                for rt in range(RT):
                    if rt >= 1:
                        for b in range(4):
                            evacA(4 * (rt - 1) + b)
                    xb = (xbuf if rt % 2 == 0 else xbuf2)[:]
                    cm = m_sb[:, rt:rt + 1]
                    v.wait_ge(par("sx", rt), 16 * (rt // 2 + 1))
                    k1 = vstep(v.reduce_sum(sums[:, rt:rt + 1], xb, axis=AX.X))
                    v.wait_ge(sem["ss1"], rt + 1)
                    v.wait_ge(sem["svA"], k1)
                    k2 = vstep(v.tensor_scalar(cm, sums[:, rt:rt + 1],
                                               1.0 / F, None, ALU.mult))
                    v.wait_ge(sem["svA"], k2)
                    k3 = vstep(v.tensor_scalar(msq[:, rt:rt + 1], cm, cm, EPS,
                                               ALU.mult, ALU.subtract))
                    v.wait_ge(sem["svA"], k3)
                    k4 = vstep(v.scalar_tensor_tensor(
                        tvv[:, rt:rt + 1], sumsq[:, rt:rt + 1], 1.0 / F,
                        msq[:, rt:rt + 1], ALU.mult, ALU.subtract))
                    v.wait_ge(sem["svA"], k4)
                    v.reciprocal(rv[:, rt:rt + 1], tvv[:, rt:rt + 1]
                                 ).then_inc(sem["sv2"], 1)
                    v.wait_ge(sem["ss2"], rt + 1)
                    if rt >= 1:
                        v.wait_ge(sem["sAtb"], 4 * rt)
                    v.tensor_scalar(xn_bf[:], xb, cm,
                                    r_sb[:, rt:rt + 1], ALU.subtract, ALU.mult
                                    ).then_inc(sem["sxn"], 1)
                for b in range(4):
                    evacA(4 * (RT - 1) + b)

                def deq(c):
                    v.wait_ge(par("s_q", c), 32 * (c // 2 + 1))
                    sc_sem, sc_val = scale_wait_sem(c)
                    v.wait_ge(sc_sem, sc_val)
                    if c >= 1:
                        v.wait_ge(sem["s_tb"], 4 * c)
                        v.wait_ge(par("s_tu", c - 1), 16 * ((c - 1) // 2 + 1))
                    cl = c % 8
                    for ti, (qn, wn) in enumerate([(qg_nat, w1g_nat), (qu_nat, w1u_nat)]):
                        soff = (2 * ti) * 256 + cl * 32
                        zoff = (2 * ti + 1) * 256 + cl * 32
                        for j in range(FJ):
                            ins = v.tensor_scalar(
                                wn[:, j * 128:(j + 1) * 128],
                                qn[c % 2][:, j * 128:(j + 1) * 128],
                                sgz[c // 8 % 2][:, zoff + j:zoff + j + 1],
                                sgz[c // 8 % 2][:, soff + j:soff + j + 1],
                                ALU.subtract, ALU.mult)
                    ins.then_inc(sem["s_vd"], 1)

                def evacsA(c):
                    for b in (0, 2):
                        v.wait_ge(sem["s_tb"], 4 * c + b + 1)
                        j0 = b * 8
                        dst = W1Tg[c % 2][:, j0 * 128:(j0 + 8) * 128]
                        v.tensor_copy(dst, trp[b % 4]).then_inc(sem["s_evA"], 1)

                def hTT(c):
                    v.wait_ge(sem["s_mm"], 2 * c + 2)
                    v.wait_ge(sem["s_ssil"], c + 1)
                    if c >= 2:
                        v.wait_ge(par("s_hw", c), 16 * (c // 2))
                    v.tensor_tensor(h_buf[c % 2], silu_t[:], psu[:], ALU.mult
                                    ).then_inc(sem["s_vh"], 1)

                deq(0)
                for c in range(NC):
                    evacsA(c)
                    if c + 1 < NC:
                        deq(c + 1)
                    if 2 <= c < 2 + FJ:
                        j = c - 2
                        if j >= 1:
                            v.wait_ge(sem["s_sd"], j)
                        v.tensor_tensor(sum01[j % 2], xnT3[:, j, 0:512],
                                        xnT3[:, j, 512:1024], ALU.add
                                        ).then_inc(sem["s_sv"], 1)
                    if c >= 1:
                        hTT(c - 1)
                hTT(NC - 1)

            @blk.scalar
            def _(s):
                for rt in range(RT):
                    s.wait_ge(par("sx", rt), 16 * (rt // 2 + 1))
                    if rt >= 1:
                        s.wait_ge(sem["sAtb"], 4 * rt)
                    s.activation(xn_bf[:], (xbuf if rt % 2 == 0 else xbuf2)[:], AF.Square,
                                 accum_out=sumsq[:, rt:rt + 1]).then_inc(sem["ss1"], 1)
                    s.wait_ge(sem["sv2"], rt + 1)
                    s.activation(r_sb[:, rt:rt + 1], rv[:, rt:rt + 1], AF.Sqrt
                                 ).then_inc(sem["ss2"], 1)

                def dtrio(j):
                    d0 = (j % 2) * 1024
                    s.activation(dmp[:, d0:d0 + 512], xnT3[:, j, 0:512],
                                 AF.Square, accum_out=dpart[:, 3 * j:3 * j + 1])
                    s.activation(dmp[:, d0 + 512:d0 + 1024], xnT3[:, j, 512:1024],
                                 AF.Square, accum_out=dpart[:, 3 * j + 2:3 * j + 3])
                    s.wait_ge(sem["s_sv"], j + 1)
                    ins = s.activation(dmp32[:], sum01[j % 2], AF.Square,
                                       accum_out=dpart[:, 3 * j + 1:3 * j + 2])
                    if j == FJ - 1:
                        ins.then_inc(sem["sAd"], 1)
                    else:
                        ins.then_inc(sem["s_sd"], 1)

                def evacsB(c):
                    for b in (1, 3):
                        s.wait_ge(sem["s_tb"], 4 * c + b + 1)
                        j0 = b * 8
                        dst = W1Tg[c % 2][:, j0 * 128:(j0 + 8) * 128]
                        s.copy(dst, trp[b % 4]).then_inc(sem["s_evB"], 1)

                def silu(c):
                    s.wait_ge(sem["s_mm"], 2 * c + 1)
                    if c >= 1:
                        s.wait_ge(sem["s_vh"], c)
                    s.activation(silu_t[:], psg[:], AF.Silu).then_inc(sem["s_ssil"], 1)

                for c in range(NC):
                    evacsB(c)
                    if c >= 1:
                        silu(c - 1)
                    if 2 <= c < 2 + FJ:
                        dtrio(c - 2)
                silu(NC - 1)

            @blk.tensor
            def _(t):
                t.wait_ge(sem["s_id"], 32)
                for rt in range(RT):
                    t.wait_ge(sem["sxn"], rt + 1)
                    for b in range(4):
                        gb = rt * 4 + b
                        if gb >= 4:
                            t.wait_ge(sem["sAev"], gb - 3)
                        for kk in range(8):
                            j = b * 8 + kk
                            ins = t.transpose(trp[gb % 4][:, kk * 128:(kk + 1) * 128],
                                              xn_bf[:, j * 128:(j + 1) * 128],
                                              ident_bf[:])
                        ins.then_inc(sem["sAtb"], 1)

                def tblock(c):
                    t.wait_ge(sem["s_vd"], c + 1)
                    if c == 0:
                        t.wait_ge(sem["sAev"], 32)
                    for b in range(4):
                        n = 2 * c + b // 2
                        if b % 2 == 0:
                            if n >= 2:
                                t.wait_ge(sem["s_evA"], n - 1)
                        else:
                            if n >= 2:
                                t.wait_ge(sem["s_evB"], n - 1)
                        j0 = b * 8
                        for kk in range(8):
                            ins = t.transpose(trp[b % 4][:, kk * 128:(kk + 1) * 128],
                                              w1g_nat[:, (j0 + kk) * 128:(j0 + kk + 1) * 128],
                                              ident_bf[:])
                        ins.then_inc(sem["s_tb"], 1)

                def mmblock(c):
                    t.wait_ge(sem["s_evA"], 2 * (c + 1))
                    t.wait_ge(sem["s_evB"], 2 * (c + 1))
                    t.wait_ge(par("s_tu", c), 16 * (c // 2 + 1))
                    if c >= 1:
                        t.wait_ge(sem["s_ssil"], c)
                        t.wait_ge(sem["s_vh"], c)
                    for (ps, W, k) in ((psg, W1Tg, 0), (psu, W1Tu, 1)):
                        for j in range(FJ):
                            for th in range(2):
                                ins = t.matmul(
                                    ps[:, th * 512:(th + 1) * 512],
                                    W[c % 2][:, j * 128:(j + 1) * 128],
                                    xnT3[:, j, th * 512:(th + 1) * 512],
                                    start=(j == 0), stop=(j == FJ - 1))
                        ins.then_inc(sem["s_mm"], 1)

                for c in range(NC):
                    tblock(c)
                    if c >= 1:
                        mmblock(c - 1)
                mmblock(NC - 1)

        with contextlib.ExitStack() as es2:
            en2 = es2.enter_context
            h_grp = _multi(en2, nc, "sbuf", "h_grp", 2, [128, GS * T], BF16)
            qd_buf = _multi(en2, nc, "sbuf", "qd_buf", 2, [128, 2048], BF16)
            w1d_nat = en2(nc.sbuf_tensor("w1d_nat", [128, 2048], BF16))
            W1dT = _multi(en2, nc, "sbuf", "W1dT", 2, [128, GS * 2048], BF16)
            sdz = _multi(en2, nc, "sbuf", "sdz", 2, [128, 2 * 8 * 32], F32)
            dacc = en2(nc.sbuf_tensor("dacc", [128, 8 * 2048], F32))
            pb4 = _multi(en2, nc, "sbuf", "pb4", 2, [128, 2048], BF16)
            xs_buf = _multi(en2, nc, "sbuf", "xs_buf", 2, [128, 512], F32)
            xp_buf = _multi(en2, nc, "sbuf", "xp_buf", 2, [128, 512], F32)
            g_sb = _multi(en2, nc, "sbuf", "g_sb", 2, [128, 512], F32)
            cA = en2(nc.sbuf_tensor("cA", [128, 512], F32))
            cB = en2(nc.sbuf_tensor("cB", [128, 512], F32))
            cC = en2(nc.psum_tensor("cC", [128, 512], F32))
            out_sb = _multi(en2, nc, "sbuf", "out_sb", 2, [128, 512], F32)
            punit = _multi(en2, nc, "psum", "punit", 4, [128, 512], F32)
            trp2 = _multi(en2, nc, "psum", "trp2", 2, [128, 1024], BF16)
            pps = en2(nc.psum_tensor("pps", [128, 128], F32))
            blk2 = en2(nc.Block())
            NSP_A = 16 * ((NC + 1) // 2)
            NSP_B = 16 * (NC // 2)

            def hw_barrier(eng):
                eng.wait_ge(sem["s_hwA"], NSP_A)
                eng.wait_ge(sem["s_hwB"], NSP_B)

            K5_OF = {}

            def xr_idx(idx):
                hh2, fc, tt = idx // 32, (idx % 32) // 8, idx % 8
                pt = (tt + 4) % 8
                col = hh2 * 2048 + fc * 512
                return hh2, fc, tt, pt, col

            @blk2.sync
            def _(sy):
                hw_barrier(sy)
                for (gg, hh, g, chs) in GG_LIST:
                    if gg >= 2:
                        sy.wait_ge(sem["s_mwd"], NW * (gg - 1))
                    src = h_sp[chs[0]:chs[0] + len(chs)].rearrange("c p t -> p c t")
                    dst = h_grp[gg % 2][:, 0:len(chs) * T]
                    dst = dst.rearrange("p (c t) -> p c t", t=T)
                    sy.dma_start(out=dst, in_=src).then_inc(par("s_hg", gg), 16)

            @blk2.gpsimd
            def _(g):
                hw_barrier(g)

                def sdz_grp(gg):
                    hh, gl = gg // NG, gg % NG
                    chs = _group_chunks(gl)
                    if gg >= 2:
                        g.wait_ge(sem["s_vdd"], PROD_LAST_K[gg - 2] + 1)
                    for ti, ten in enumerate([sd_r, zd_r]):
                        src = ten[chs[0]:chs[0] + len(chs)].rearrange("c p g -> p c g")
                        dst = sdz[gg % 2][:, ti * 256:ti * 256 + len(chs) * 32]
                        dst = dst.rearrange("p (c g) -> p c g", g=32)
                        g.dma_start(out=dst, in_=src).then_inc(par("s_sdd", gg), 16)

                def qd_dma(k):
                    gg, c = PROD[k]
                    hh = gg // NG
                    if k >= 2:
                        g.wait_ge(sem["s_vdd"], k - 1)
                    src = wd_r[c, :, 16 * hh:16 * hh + 16, :]
                    dst = qd_buf[k % 2].rearrange("p (a b) -> p a b", b=128)
                    g.dma_start(out=dst, in_=src,
                                max_dma_last_dim=8192).then_inc(par("s_qd", k), 16)

                def deq_d_g(k):
                    gg2, c2 = PROD[k]
                    hh2 = gg2 // NG
                    g.wait_ge(par("s_qd", k), 16 * (k // 2 + 1))
                    g.wait_ge(par("s_sdd", gg2), 32 * (gg2 // 2 + 1))
                    g.wait_ge(sem["s_tbd"], 2 * k)
                    cl2 = c2 % 8
                    for j in range(16):
                        jj = 16 * hh2 + j
                        ins = g.tensor_scalar(
                            w1d_nat[:, j * 128:(j + 1) * 128],
                            qd_buf[k % 2][:, j * 128:(j + 1) * 128],
                            sdz[gg2 % 2][:, 256 + cl2 * 32 + jj:256 + cl2 * 32 + jj + 1],
                            sdz[gg2 % 2][:, cl2 * 32 + jj:cl2 * 32 + jj + 1],
                            ALU.subtract, ALU.mult)
                    ins.then_inc(sem["s_vdd"], 1)

                sdz_grp(0)
                for k in range(PROD_FIRST_K[0], PROD_LAST_K[0] + 1):
                    qd_dma(k)
                for (gg, hh, gl, chs) in GG_LIST:
                    if gg + 1 < len(GG_LIST):
                        sdz_grp(gg + 1)
                        kf = PROD_FIRST_K[gg + 1]
                        kl = PROD_LAST_K[gg + 1]
                        for k in range(kf, kl + 1):
                            qd_dma(k)
                    if gg == 10 or gg == 21:
                        hh2 = 0 if gg == 10 else 1
                        if hh2 == 0:
                            g.wait_ge(sem["s_pd"], 16)
                        else:
                            g.wait_ge(sem["s_cmb"], 32)
                        for bi, ch in enumerate((0, 2)):
                            src = p_dram[ch:ch + 1, hh2 * 2048:(hh2 + 1) * 2048]
                            src = src.partition_broadcast(128)[:, 0, :]
                            g.dma_start(out=pb4[bi], in_=src,
                                        max_dma_last_dim=8192
                                        ).then_inc(sem["s_pb"], 16)
                g.wait_ge(sem["s_outA"], 16 * 32)
                g.wait_ge(sem["s_outB"], 16 * 32)

            @blk2.vector
            def _(v):
                hw_barrier(v)
                v.tensor_tensor(negrm[:], m_sb[:], r_sb[:], ALU.mult
                                ).then_inc(sem["s_cch"], 1)
                ccn0_rm = 1
                v.wait_ge(sem["s_cch"], ccn0_rm)
                v.tensor_scalar(negrm[:], negrm[:], -1.0, None, ALU.mult
                                ).then_inc(sem["s_rm"], 1)

                def p_chain():
                    v.wait_ge(sem["spg"], 16)
                    dA = dsum[:, 0:96:3]
                    dP = dsum[:, 1:96:3]
                    dB = dsum[:, 2:96:3]
                    v.tensor_tensor(p_q[:, 0:32], dA, dB, ALU.add
                                    ).then_inc(sem["s_pch"], 1)
                    v.wait_ge(sem["s_pch"], 1)
                    v.tensor_tensor(p_q[:, 32:64], p_q[:, 0:32], dP, ALU.subtract
                                    ).then_inc(sem["s_pch"], 1)
                    v.wait_ge(sem["s_pch"], 2)
                    v.scalar_tensor_tensor(p_q[:, 64:96], p_q[:, 32:64], 0.5, dA,
                                           ALU.mult, ALU.add).then_inc(sem["s_pch"], 1)
                    v.scalar_tensor_tensor(p_q[:, 96:128], p_q[:, 32:64], 0.5, dB,
                                           ALU.mult, ALU.add).then_inc(sem["s_pv1"], 1)

                def deq_d(k):
                    gg, c = PROD[k]
                    hh = gg // NG
                    v.wait_ge(par("s_qd", k), 16 * (k // 2 + 1))
                    v.wait_ge(par("s_sdd", gg), 32 * (gg // 2 + 1))
                    if k >= 1:
                        v.wait_ge(sem["s_tbd"], 2 * k)
                    cl = c % 8
                    for j in range(16):
                        jj = 16 * hh + j
                        ins = v.tensor_scalar(
                            w1d_nat[:, j * 128:(j + 1) * 128],
                            qd_buf[k % 2][:, j * 128:(j + 1) * 128],
                            sdz[gg % 2][:, 256 + cl * 32 + jj:256 + cl * 32 + jj + 1],
                            sdz[gg % 2][:, cl * 32 + jj:cl * 32 + jj + 1],
                            ALU.subtract, ALU.mult)
                    ins.then_inc(sem["s_vdd"], 1)

                def flush(gg, w):
                    W = NW * gg + w
                    v.wait_ge(sem["s_mwd"], W + 1)
                    first = (gg % NG == 0)
                    if first and gg >= NG:
                        v.wait_ge(sem["s_gs"], 2 * w + 2)
                    for u in range(2):
                        tt, fc = UNITS[2 * w + u]
                        dst = dacc[:, tt * 2048 + fc * 512:tt * 2048 + fc * 512 + 512]
                        if first:
                            ins = v.tensor_copy(dst, punit[(w % 2) * 2 + u])
                        else:
                            ins = v.tensor_tensor(dst, dst, punit[(w % 2) * 2 + u],
                                                  ALU.add)
                    ins.then_inc(sem["s_fld"], 1)

                ccn = [1]

                def cstep(ins):
                    ins.then_inc(sem["s_cch"], 1)
                    ccn[0] += 1
                    return ccn[0]

                def cmb(idx):
                    hh, fc, tt, pt, col = xr_idx(idx)
                    v.wait_ge(sem["s_pb"], 32 * (hh + 1))
                    v.wait_ge(par("s_xr", idx), 32 * (idx // 2 + 1))
                    v.wait_ge(sem["s_gs"], idx + 1)
                    v.wait_ge(sem["s_cab"], 2 * (idx + 1))
                    if idx >= 2:
                        v.wait_ge(par("s_out", idx), 16 * (idx // 2))
                    fs = slice(fc * 512, (fc + 1) * 512)
                    pbX = pb4[0][:, fs] if tt < 4 else pb4[1][:, fs]
                    first, second = (cA, cB) if tt < 4 else (cB, cA)
                    k3 = cstep(v.tensor_tensor(cC[:], first[:], second[:],
                                               ALU.subtract))
                    v.wait_ge(sem["s_cch"], k3)
                    k4 = cstep(v.tensor_tensor(first[:], cC[:], pbX, ALU.mult))
                    v.wait_ge(sem["s_cch"], k4)
                    k5 = cstep(v.scalar_tensor_tensor(cC[:], second[:], 2.0,
                                                      first[:], ALU.mult, ALU.add))
                    K5_OF[idx] = k5
                    v.wait_ge(sem["s_cch"], k5)
                    k6 = cstep(v.tensor_tensor(first[:], cC[:], g_sb[idx % 2],
                                               ALU.mult))
                    v.wait_ge(sem["s_cch"], k6)
                    v.tensor_tensor(out_sb[idx % 2], first[:], xs_buf[idx % 2],
                                    ALU.add).then_inc(sem["s_cmb"], 1)

                for k in range(PROD_FIRST_K[0], PROD_LAST_K[0] + 1):
                    deq_d(k)
                for (gg, hh, gl, chs) in GG_LIST:
                    nxt = (list(range(PROD_FIRST_K[gg + 1], PROD_LAST_K[gg + 1] + 1))
                           if gg + 1 < len(GG_LIST) else [])
                    for w in range(NW):
                        if w < len(nxt) and nxt[w] not in GDEQ:
                            deq_d(nxt[w])
                        if gg == NG and w >= 8:
                            cmb(2 * w)
                            cmb(2 * w + 1)
                        flush(gg, w)
                        if gg == 21 and w >= 1:
                            cmb(32 + 2 * (w - 1))
                            cmb(32 + 2 * (w - 1) + 1)
                    for k in nxt[NW:]:
                        if k not in GDEQ:
                            deq_d(k)
                    if gg == 7:
                        p_chain()
                    for idx in cmb_idxs_at(gg):
                        cmb(idx)
                for idx in range(62, 64):
                    cmb(idx)

            @blk2.scalar
            def _(s):
                hw_barrier(s)

                def p_sig():
                    s.wait_ge(sem["s_pv1"], 1)
                    s.activation(p_s[:, 0:32], p_q[:, 64:96],
                                 AF.Sigmoid, scale=1.0 / 64)
                    s.activation(p_s[:, 32:64], p_q[:, 64:96],
                                 AF.Sigmoid, scale=-1.0 / 64)
                    s.activation(p_s[:, 64:96], p_q[:, 96:128],
                                 AF.Sigmoid, scale=-1.0 / 64)
                    s.activation(p_s[:, 96:128], p_q[:, 96:128],
                                 AF.Sigmoid, scale=1.0 / 64
                                 ).then_inc(sem["s_sch"], 1)
                    s.wait_ge(sem["s_sch"], 1)
                    s.add(p_s[:, 0:32], p_s[:, 0:32], 1.0)
                    s.add(p_s[:, 96:128], p_s[:, 96:128], 1.0
                          ).then_inc(sem["s_ps1"], 1)

                def p_store():
                    s.wait_ge(sem["s_ptr"], 1)
                    s.copy(p_qT[:], pps[:]).then_inc(sem["s_sch"], 1)
                    s.wait_ge(sem["s_sch"], 2)
                    pd = p_dram.rearrange("c (j q) -> (c j) q", q=128)
                    s.dma_start(out=pd, in_=p_qT[:]).then_inc(sem["s_pd"], 16)

                def evac_dA(k):
                    gg, c = PROD[k]
                    slot = c % 8
                    s.wait_ge(sem["s_tbd"], 2 * k + 1)
                    if gg >= 2:
                        s.wait_ge(sem["s_mwd"], NW * (gg - 1))
                    dst = W1dT[gg % 2][:, slot * 2048:slot * 2048 + 1024]
                    s.copy(dst, trp2[0]).then_inc(sem["s_evdA"], 1)

                def evac_dB(k):
                    gg, c = PROD[k]
                    slot = c % 8
                    s.wait_ge(sem["s_tbd"], 2 * k + 2)
                    if gg >= 2:
                        s.wait_ge(sem["s_mwd"], NW * (gg - 1))
                    dst = W1dT[gg % 2][:, slot * 2048 + 1024:slot * 2048 + 2048]
                    s.copy(dst, trp2[1]).then_inc(sem["s_evdB"], 1)

                def xr_pair(idx):
                    hh2, fc, tt, pt, col = xr_idx(idx)
                    s.dma_start(
                        out=xs_buf[idx % 2],
                        in_=x_rows[tt * 128:(tt + 1) * 128, col:col + 512]
                    ).then_inc(par("s_xr", idx), 16)
                    s.dma_start(
                        out=xp_buf[idx % 2],
                        in_=x_rows[pt * 128:(pt + 1) * 128, col:col + 512]
                    ).then_inc(par("s_xr", idx), 16)

                def sg_out(idx):
                    hh, fc, tt, pt, col = xr_idx(idx)
                    ui = fc * 8 + tt
                    gg_last = NG - 1 + hh * NG
                    s.wait_ge(sem["s_fld"], NW * gg_last + ui // 2 + 1)
                    if idx >= 2:
                        s.wait_ge(sem["s_cmb"], idx - 1)
                    src = dacc[:, tt * 2048 + fc * 512:tt * 2048 + fc * 512 + 512]
                    s.activation(g_sb[idx % 2], src, AF.Silu
                                 ).then_inc(sem["s_gs"], 1)
                    if idx == 0:
                        s.wait_ge(sem["s_rm"], 1)
                    s.wait_ge(par("s_xr", idx), 32 * (idx // 2 + 1))
                    if idx >= 1:
                        s.wait_ge(sem["s_cch"], K5_OF[idx - 1])
                    s.activation(cA[:], xs_buf[idx % 2], AF.Identity,
                                 scale=r_sb[:, tt:tt + 1],
                                 bias=negrm[:, tt:tt + 1]).then_inc(sem["s_cab"], 1)
                    s.activation(cB[:], xp_buf[idx % 2], AF.Identity,
                                 scale=r_sb[:, pt:pt + 1],
                                 bias=negrm[:, pt:pt + 1]).then_inc(sem["s_cab"], 1)
                    s.wait_ge(sem["s_cmb"], idx + 1)
                    s.dma_start(out=out_rows[tt * 128:(tt + 1) * 128,
                                             col:col + 512],
                                in_=out_sb[idx % 2]).then_inc(par("s_out", idx), 16)
                    if idx + 2 < 64:
                        xr_pair(idx + 2)

                for k in range(PROD_FIRST_K[0], PROD_LAST_K[0] + 1):
                    evac_dA(k)
                    evac_dB(k)
                for (gg, hh, gl, chs) in GG_LIST:
                    if gg + 1 < len(GG_LIST):
                        for k in range(PROD_FIRST_K[gg + 1], PROD_LAST_K[gg + 1] + 1):
                            evac_dA(k)
                            evac_dB(k)
                    if gg == 8:
                        p_sig()
                    if gg == 9:
                        p_store()
                        xr_pair(0)
                        xr_pair(1)
                    for idx in cmb_idxs_at(gg):
                        sg_out(idx)
                    if gg == NG:
                        for idx in range(16, 32):
                            sg_out(idx)
                for idx in range(32, 64):
                    sg_out(idx)

            @blk2.tensor
            def _(t):
                hw_barrier(t)

                def tb_d(k):
                    gg, c = PROD[k]
                    t.wait_ge(sem["s_vdd"], k + 1)
                    for b in range(2):
                        if k >= 1:
                            t.wait_ge(sem["s_evdA" if b == 0 else "s_evdB"], k)
                        for kk in range(8):
                            j = b * 8 + kk
                            ins = t.transpose(trp2[b][:, kk * 128:(kk + 1) * 128],
                                              w1d_nat[:, j * 128:(j + 1) * 128],
                                              ident_bf[:])
                        ins.then_inc(sem["s_tbd"], 1)

                def wave(gg, gl, chs, w):
                    W = NW * gg + w
                    if W >= 2:
                        t.wait_ge(sem["s_fld"], W - 1)
                    if w == 0:
                        t.wait_ge(par("s_hg", gg), 16 * (gg // 2 + 1))
                    k0 = PROD_FIRST_K[gg]
                    for ci in range(len(chs)):
                        if w == 0:
                            t.wait_ge(sem["s_evdA"], k0 + ci + 1)
                            t.wait_ge(sem["s_evdB"], k0 + ci + 1)
                        for u in range(2):
                            tt, fc = UNITS[2 * w + u]
                            ins = t.matmul(
                                punit[(w % 2) * 2 + u],
                                h_grp[gg % 2][:, ci * T + tt * 128:ci * T + tt * 128 + 128],
                                W1dT[gg % 2][:, ci * 2048 + fc * 512:ci * 2048 + fc * 512 + 512],
                                start=(ci == 0), stop=(ci == len(chs) - 1))
                    ins.then_inc(sem["s_mwd"], 1)

                for k in range(PROD_FIRST_K[0], PROD_LAST_K[0] + 1):
                    tb_d(k)
                for (gg, hh, gl, chs) in GG_LIST:
                    nxt = (list(range(PROD_FIRST_K[gg + 1], PROD_LAST_K[gg + 1] + 1))
                           if gg + 1 < len(GG_LIST) else [])
                    for w in range(NW):
                        wave(gg, gl, chs, w)
                        if w < len(nxt):
                            tb_d(nxt[w])
                    for k in nxt[NW:]:
                        tb_d(k)
                    if gg == 8:
                        t.wait_ge(sem["s_ps1"], 1)
                        t.transpose(pps[:], p_s[:], ident_f[:]).then_inc(sem["s_ptr"], 1)

    return nc


def kernel(L0_attn_x, L0_ffn_gate_qweight, L0_ffn_gate_scales, L0_ffn_gate_zeros,
           L0_ffn_up_qweight, L0_ffn_up_scales, L0_ffn_up_zeros,
           L0_ffn_down_qweight, L0_ffn_down_scales, L0_ffn_down_zeros):
    x = np.ascontiguousarray(np.asarray(L0_attn_x, dtype=np.float32))
    qg = np.ascontiguousarray(np.asarray(L0_ffn_gate_qweight, dtype=np.int32))
    qu = np.ascontiguousarray(np.asarray(L0_ffn_up_qweight, dtype=np.int32))
    qd = np.asarray(L0_ffn_down_qweight, dtype=np.int32)
    wd_r = np.ascontiguousarray(
        qd.reshape(FJ, 128, NC, 128).transpose(2, 1, 0, 3))
    sd = np.asarray(L0_ffn_down_scales, dtype=np.float32)
    zd = np.asarray(L0_ffn_down_zeros, dtype=np.float32)
    sd_r = np.ascontiguousarray(sd.reshape(FJ, 128, NC).transpose(2, 1, 0))
    zd_r = np.ascontiguousarray(zd.reshape(FJ, 128, NC).transpose(2, 1, 0))
    shared = {
        "qg": qg, "qu": qu, "wd_r": wd_r,
        "sg": np.ascontiguousarray(np.asarray(L0_ffn_gate_scales, np.float32)),
        "zg": np.ascontiguousarray(np.asarray(L0_ffn_gate_zeros, np.float32)),
        "su": np.ascontiguousarray(np.asarray(L0_ffn_up_scales, np.float32)),
        "zu": np.ascontiguousarray(np.asarray(L0_ffn_up_zeros, np.float32)),
        "sd_r": sd_r, "zd_r": zd_r,
        "ident": np.eye(128, dtype=np.float32),
    }
    nc = build_nc()
    in_maps = []
    for k in range(NCORES):
        m = dict(shared)
        m["x_rows"] = np.ascontiguousarray(
            x[:, k * 512:(k + 1) * 512, :]).reshape(T, F)
        in_maps.append(m)
    res = run_bass_kernel_spmd(nc, in_maps, core_ids=list(range(NCORES)))
    out = np.empty((2, 4096, 4096), np.float32)
    for k in range(NCORES):
        out[:, k * 512:(k + 1) * 512, :] = res.results[k]["out_rows"].reshape(2, 512, F)
    return out


# version 13 (final)
# speedup vs baseline: 1.1024x; 1.0303x over previous
import numpy as np
import concourse.bass as bass
import concourse.mybir as mybir
from concourse.bass_utils import run_bass_kernel_spmd

F32 = mybir.dt.float32
BF16 = mybir.dt.bfloat16
I32 = mybir.dt.int32
AF = mybir.ActivationFunctionType
ALU = mybir.AluOpType
AX = mybir.AxisListType

NCORES = 8
T = 1024
F = 4096
NI = 11008
NC = 86
FJ = 32
RT = 8
GS = 8
NG = 11
NW = 16
EPS = 1e-5


def _group_chunks(g):
    return list(range(8 * g, min(8 * g + 8, NC)))


GG_LIST = []
for _hh in range(2):
    for _g in range(NG):
        GG_LIST.append((len(GG_LIST), _hh, _g, _group_chunks(_g)))
PROD = [(gg, c) for (gg, _hh, _g, chs) in GG_LIST for c in chs]
PROD_FIRST_K = {}
PROD_LAST_K = {}
for _k, (_gg, _c) in enumerate(PROD):
    PROD_FIRST_K.setdefault(_gg, _k)
    PROD_LAST_K[_gg] = _k
CUM_CHUNKS = {}
_cum = 0
for (_gg, _hh, _g, chs) in GG_LIST:
    _cum += len(chs)
    CUM_CHUNKS[_gg] = _cum

UNITS = [(tt, fc) for fc in range(4) for tt in range(8)]
GDEQ = set()


def cmb_idxs_at(gg):
    if gg == 10:
        return list(range(0, 16))
    return []


class MultiBuf:

    def __init__(self, ts):
        self.ts = ts

    def __getitem__(self, i):
        return self.ts[i][:]


def _multi(en, nc, kind, name, n, shape, dt):
    mk = nc.sbuf_tensor if kind == "sbuf" else nc.psum_tensor
    return MultiBuf([en(mk(f"{name}{i}", shape, dt)) for i in range(n)])


def build_nc():
    nc = bass.Bass("TRN2", target_bir_lowering=False)

    x_rows = nc.dram_tensor("x_rows", [T, F], F32, kind="ExternalInput").ap()
    qg = nc.dram_tensor("qg", [NI, F], I32, kind="ExternalInput").ap()
    qu = nc.dram_tensor("qu", [NI, F], I32, kind="ExternalInput").ap()
    wd_r = nc.dram_tensor("wd_r", [NC, 128, FJ, 128], I32, kind="ExternalInput").ap()
    sg = nc.dram_tensor("sg", [NI, 32], F32, kind="ExternalInput").ap()
    zg = nc.dram_tensor("zg", [NI, 32], F32, kind="ExternalInput").ap()
    su = nc.dram_tensor("su", [NI, 32], F32, kind="ExternalInput").ap()
    zu = nc.dram_tensor("zu", [NI, 32], F32, kind="ExternalInput").ap()
    sd_r = nc.dram_tensor("sd_r", [NC, 128, FJ], F32, kind="ExternalInput").ap()
    zd_r = nc.dram_tensor("zd_r", [NC, 128, FJ], F32, kind="ExternalInput").ap()
    ident = nc.dram_tensor("ident", [128, 128], F32, kind="ExternalInput").ap()
    out_rows = nc.dram_tensor("out_rows", [T, F], F32, kind="ExternalOutput").ap()

    h_sp = nc.dram_tensor("h_sp", [NC, 128, T], BF16).ap()
    din = nc.dram_tensor("din", [128, 96], F32)
    dout = nc.dram_tensor("dout", [128, 96], F32)
    p_dram = nc.dram_tensor("p_dram", [4, F], F32).ap()

    import contextlib
    est = contextlib.ExitStack()
    with est:
        sem = {}
        for name in ["sxA", "sxB", "ss1", "sv2", "ss2", "sxn", "sAtb", "sAev",
                     "sAd", "sdin", "scc", "spg", "s_id", "svA",
                     "s_sv", "s_sd", "s_pch", "s_sch", "s_cch",
                     "s_qA", "s_qB", "s_scA", "s_scB", "s_vd", "s_tb", "s_evA", "s_evB", "s_tuA", "s_tuB",
                     "s_mm", "s_ssil", "s_vh", "s_hwA", "s_hwB",
                     "s_hgA", "s_hgB", "s_qdA", "s_qdB", "s_sddA", "s_sddB",
                     "s_vdd", "s_tbd", "s_evdA",
                     "s_evdB", "s_mwd", "s_fld",
                     "s_pv1", "s_ps1", "s_ptr", "s_pd", "s_pb", "s_rm", "s_cab",
                     "s_xrA", "s_xrB", "s_gs", "s_cmb", "s_outA", "s_outB"]:
            sem[name] = est.enter_context(nc.semaphore(name))

        def par(base, i):
            return sem[base + ("A" if i % 2 == 0 else "B")]

        ident_bf = est.enter_context(nc.sbuf_tensor("ident_bf", [128, 128], BF16))
        ident_f = est.enter_context(nc.sbuf_tensor("ident_f", [128, 128], F32))
        m_sb = est.enter_context(nc.sbuf_tensor("m_sb", [128, 8], F32))
        r_sb = est.enter_context(nc.sbuf_tensor("r_sb", [128, 8], F32))
        dpart = est.enter_context(nc.sbuf_tensor("dpart", [128, 96], F32))
        dsum = est.enter_context(nc.sbuf_tensor("dsum", [128, 96], F32))
        p_q = est.enter_context(nc.sbuf_tensor("p_q", [128, 128], F32))
        p_s = est.enter_context(nc.sbuf_tensor("p_s", [128, 128], F32))
        p_qT = est.enter_context(nc.sbuf_tensor("p_qT", [128, 128], F32))
        negrm = est.enter_context(nc.sbuf_tensor("negrm", [128, 8], F32))

        with contextlib.ExitStack() as es1:
            en1 = es1.enter_context
            xnT = en1(nc.sbuf_tensor("xnT", [128, FJ * T], BF16))
            xbuf = en1(nc.sbuf_tensor("xbuf", [128, F], F32))
            xn_bf = en1(nc.sbuf_tensor("xn_bf", [128, F], BF16))
            sums = en1(nc.sbuf_tensor("sums", [128, 8], F32))
            sumsq = en1(nc.sbuf_tensor("sumsq", [128, 8], F32))
            msq = en1(nc.sbuf_tensor("msq", [128, 8], F32))
            tvv = en1(nc.sbuf_tensor("tvv", [128, 8], F32))
            rv = en1(nc.sbuf_tensor("rv", [128, 8], F32))
            sum01 = _multi(en1, nc, "sbuf", "sum01", 2, [128, 512], F32)
            dmp = en1(nc.sbuf_tensor("dmp", [128, 2048], BF16))
            dmp32 = en1(nc.sbuf_tensor("dmp32", [128, 512], F32))
            qg_nat = _multi(en1, nc, "sbuf", "qg_nat", 2, [128, F], BF16)
            qu_nat = _multi(en1, nc, "sbuf", "qu_nat", 2, [128, F], BF16)
            w1g_nat = en1(nc.sbuf_tensor("w1g_nat", [128, F], BF16))
            w1u_nat = en1(nc.sbuf_tensor("w1u_nat", [128, F], BF16))
            es1a = contextlib.ExitStack()
            xbuf2 = es1a.enter_context(nc.sbuf_tensor("xbuf2", [128, F], F32))
            es1a.close()
            W1Tg = _multi(en1, nc, "sbuf", "W1Tg", 2, [128, F], BF16)
            W1Tu = _multi(en1, nc, "sbuf", "W1Tu", 2, [128, F], BF16)
            sgz = _multi(en1, nc, "sbuf", "sgz", 2, [128, 4 * 8 * 32], F32)
            silu_t = en1(nc.sbuf_tensor("silu_t", [128, T], F32))
            h_buf = _multi(en1, nc, "sbuf", "h_buf", 2, [128, T], BF16)
            psg = en1(nc.psum_tensor("psg", [128, T], F32))
            psu = en1(nc.psum_tensor("psu", [128, T], F32))
            trp = _multi(en1, nc, "psum", "trp", 4, [128, 1024], BF16)
            blk = en1(nc.Block())
            xnT3 = xnT[:].rearrange("p (j t) -> p j t", t=T)

            def scale_wait_sem(c):
                blk_ = c // 8
                return par("s_sc", blk_), 64 * (blk_ // 2 + 1)

            @blk.sync
            def _(sy):
                for rt in range(RT):
                    if rt >= 2:
                        sy.wait_ge(sem["sxn"], rt - 1)
                    xb_t = xbuf if rt % 2 == 0 else xbuf2
                    sy.dma_start(out=xb_t[:], in_=x_rows[rt * 128:(rt + 1) * 128, :]
                                 ).then_inc(par("sx", rt), 16)
                for c in range(NC):
                    sy.wait_ge(sem["s_vd"], c + 1)
                    if c >= 2:
                        sy.wait_ge(sem["s_mm"], 2 * (c - 1))
                    sy.dma_start(
                        out=W1Tu[c % 2].rearrange("p (j q) -> p j q", q=128),
                        in_=w1u_nat[:].rearrange("p (j q) -> p j q", q=128),
                        transpose=True).then_inc(par("s_tu", c), 16)
                    if c >= 2:
                        sy.wait_ge(sem["s_vh"], c - 1)
                        sy.dma_start(out=h_sp[c - 2], in_=h_buf[c % 2]
                                     ).then_inc(par("s_hw", c - 2), 16)
                for c in (NC - 2, NC - 1):
                    sy.wait_ge(sem["s_vh"], c + 1)
                    sy.dma_start(out=h_sp[c], in_=h_buf[c % 2]).then_inc(par("s_hw", c), 16)

            @blk.gpsimd
            def _(g):
                g.dma_start(out=ident_bf[:], in_=ident).then_inc(sem["s_id"], 16)
                g.dma_start(out=ident_f[:], in_=ident).then_inc(sem["s_id"], 16)

                def scale_grp(blkidx):
                    n = min(8, NC - 8 * blkidx)
                    if blkidx >= 2:
                        g.wait_ge(sem["s_vd"], 8 * (blkidx - 2) + 8)
                    for ti, ten in enumerate([sg, zg, su, zu]):
                        src = ten[8 * blkidx * 128:(8 * blkidx + n) * 128, :]
                        src = src.rearrange("(c p) g -> p c g", p=128)
                        dst = sgz[blkidx % 2][:, ti * 256:ti * 256 + n * 32]
                        dst = dst.rearrange("p (c g) -> p c g", g=32)
                        g.dma_start(out=dst, in_=src).then_inc(par("s_sc", blkidx), 16)

                scale_grp(0)
                for c in range(NC):
                    if c % 8 == 0 and c > 0:
                        scale_grp(c // 8)
                    if c >= 2:
                        g.wait_ge(sem["s_vd"], c - 1)
                    g.dma_start(out=qg_nat[c % 2], in_=qg[c * 128:(c + 1) * 128, :],
                                max_dma_last_dim=8192).then_inc(par("s_q", c), 16)
                    g.dma_start(out=qu_nat[c % 2], in_=qu[c * 128:(c + 1) * 128, :],
                                max_dma_last_dim=8192).then_inc(par("s_q", c), 16)
                g.wait_ge(sem["sAd"], 1)
                g.dma_start(out=din.ap(), in_=dpart[:]).then_inc(sem["sdin"], 16)
                g.wait_ge(sem["sdin"], 16)
                g.collective_compute(
                    "AllReduce", ALU.add,
                    replica_groups=[list(range(NCORES))],
                    ins=[din.ap()], outs=[dout.ap()],
                ).then_inc(sem["scc"], 1)
                g.wait_ge(sem["scc"], 1)
                g.dma_start(out=dsum[:], in_=dout.ap()).then_inc(sem["spg"], 16)

            @blk.vector
            def _(v):
                svn = [0]

                def vstep(ins):
                    ins.then_inc(sem["svA"], 1)
                    svn[0] += 1
                    return svn[0]

                def evacA(gb):
                    rt_, b_ = gb // 4, gb % 4
                    v.wait_ge(sem["sAtb"], gb + 1)
                    dst = xnT3[:, b_ * 8:(b_ + 1) * 8, rt_ * 128:(rt_ + 1) * 128]
                    src = trp[gb % 4].rearrange("p (a q) -> p a q", q=128)
                    v.tensor_copy(dst, src).then_inc(sem["sAev"], 1)

                for rt in range(RT):
                    if rt >= 1:
                        for b in range(4):
                            evacA(4 * (rt - 1) + b)
                    xb = (xbuf if rt % 2 == 0 else xbuf2)[:]
                    cm = m_sb[:, rt:rt + 1]
                    v.wait_ge(par("sx", rt), 16 * (rt // 2 + 1))
                    k1 = vstep(v.reduce_sum(sums[:, rt:rt + 1], xb, axis=AX.X))
                    v.wait_ge(sem["ss1"], rt + 1)
                    v.wait_ge(sem["svA"], k1)
                    k2 = vstep(v.tensor_scalar(cm, sums[:, rt:rt + 1],
                                               1.0 / F, None, ALU.mult))
                    v.wait_ge(sem["svA"], k2)
                    k3 = vstep(v.tensor_scalar(msq[:, rt:rt + 1], cm, cm, EPS,
                                               ALU.mult, ALU.subtract))
                    v.wait_ge(sem["svA"], k3)
                    k4 = vstep(v.scalar_tensor_tensor(
                        tvv[:, rt:rt + 1], sumsq[:, rt:rt + 1], 1.0 / F,
                        msq[:, rt:rt + 1], ALU.mult, ALU.subtract))
                    v.wait_ge(sem["svA"], k4)
                    v.reciprocal(rv[:, rt:rt + 1], tvv[:, rt:rt + 1]
                                 ).then_inc(sem["sv2"], 1)
                    v.wait_ge(sem["ss2"], rt + 1)
                    if rt >= 1:
                        v.wait_ge(sem["sAtb"], 4 * rt)
                    v.tensor_scalar(xn_bf[:], xb, cm,
                                    r_sb[:, rt:rt + 1], ALU.subtract, ALU.mult
                                    ).then_inc(sem["sxn"], 1)
                for b in range(4):
                    evacA(4 * (RT - 1) + b)

                def deq(c):
                    v.wait_ge(par("s_q", c), 32 * (c // 2 + 1))
                    sc_sem, sc_val = scale_wait_sem(c)
                    v.wait_ge(sc_sem, sc_val)
                    if c >= 1:
                        v.wait_ge(sem["s_tb"], 4 * c)
                        v.wait_ge(par("s_tu", c - 1), 16 * ((c - 1) // 2 + 1))
                    cl = c % 8
                    for ti, (qn, wn) in enumerate([(qg_nat, w1g_nat), (qu_nat, w1u_nat)]):
                        soff = (2 * ti) * 256 + cl * 32
                        zoff = (2 * ti + 1) * 256 + cl * 32
                        for j in range(FJ):
                            ins = v.tensor_scalar(
                                wn[:, j * 128:(j + 1) * 128],
                                qn[c % 2][:, j * 128:(j + 1) * 128],
                                sgz[c // 8 % 2][:, zoff + j:zoff + j + 1],
                                sgz[c // 8 % 2][:, soff + j:soff + j + 1],
                                ALU.subtract, ALU.mult)
                    ins.then_inc(sem["s_vd"], 1)

                def evacsA(c):
                    for b in (0, 2):
                        v.wait_ge(sem["s_tb"], 4 * c + b + 1)
                        j0 = b * 8
                        dst = W1Tg[c % 2][:, j0 * 128:(j0 + 8) * 128]
                        v.tensor_copy(dst, trp[b % 4]).then_inc(sem["s_evA"], 1)

                def hTT(c):
                    v.wait_ge(sem["s_mm"], 2 * c + 2)
                    v.wait_ge(sem["s_ssil"], c + 1)
                    if c >= 2:
                        v.wait_ge(par("s_hw", c), 16 * (c // 2))
                    v.tensor_tensor(h_buf[c % 2], silu_t[:], psu[:], ALU.mult
                                    ).then_inc(sem["s_vh"], 1)

                deq(0)
                for c in range(NC):
                    evacsA(c)
                    if c + 1 < NC:
                        deq(c + 1)
                    if 2 <= c < 2 + FJ:
                        j = c - 2
                        if j >= 1:
                            v.wait_ge(sem["s_sd"], j)
                        v.tensor_tensor(sum01[j % 2], xnT3[:, j, 0:512],
                                        xnT3[:, j, 512:1024], ALU.add
                                        ).then_inc(sem["s_sv"], 1)
                    if c >= 1:
                        hTT(c - 1)
                hTT(NC - 1)

            @blk.scalar
            def _(s):
                for rt in range(RT):
                    s.wait_ge(par("sx", rt), 16 * (rt // 2 + 1))
                    if rt >= 1:
                        s.wait_ge(sem["sAtb"], 4 * rt)
                    s.activation(xn_bf[:], (xbuf if rt % 2 == 0 else xbuf2)[:], AF.Square,
                                 accum_out=sumsq[:, rt:rt + 1]).then_inc(sem["ss1"], 1)
                    s.wait_ge(sem["sv2"], rt + 1)
                    s.activation(r_sb[:, rt:rt + 1], rv[:, rt:rt + 1], AF.Sqrt
                                 ).then_inc(sem["ss2"], 1)

                def dtrio(j):
                    d0 = (j % 2) * 1024
                    s.activation(dmp[:, d0:d0 + 512], xnT3[:, j, 0:512],
                                 AF.Square, accum_out=dpart[:, 3 * j:3 * j + 1])
                    s.activation(dmp[:, d0 + 512:d0 + 1024], xnT3[:, j, 512:1024],
                                 AF.Square, accum_out=dpart[:, 3 * j + 2:3 * j + 3])
                    s.wait_ge(sem["s_sv"], j + 1)
                    ins = s.activation(dmp32[:], sum01[j % 2], AF.Square,
                                       accum_out=dpart[:, 3 * j + 1:3 * j + 2])
                    if j == FJ - 1:
                        ins.then_inc(sem["sAd"], 1)
                    else:
                        ins.then_inc(sem["s_sd"], 1)

                def evacsB(c):
                    for b in (1, 3):
                        s.wait_ge(sem["s_tb"], 4 * c + b + 1)
                        j0 = b * 8
                        dst = W1Tg[c % 2][:, j0 * 128:(j0 + 8) * 128]
                        s.copy(dst, trp[b % 4]).then_inc(sem["s_evB"], 1)

                def silu(c):
                    s.wait_ge(sem["s_mm"], 2 * c + 1)
                    if c >= 1:
                        s.wait_ge(sem["s_vh"], c)
                    s.activation(silu_t[:], psg[:], AF.Silu).then_inc(sem["s_ssil"], 1)

                for c in range(NC):
                    evacsB(c)
                    if c >= 1:
                        silu(c - 1)
                    if 2 <= c < 2 + FJ:
                        dtrio(c - 2)
                silu(NC - 1)

            @blk.tensor
            def _(t):
                t.wait_ge(sem["s_id"], 32)
                for rt in range(RT):
                    t.wait_ge(sem["sxn"], rt + 1)
                    for b in range(4):
                        gb = rt * 4 + b
                        if gb >= 4:
                            t.wait_ge(sem["sAev"], gb - 3)
                        for kk in range(8):
                            j = b * 8 + kk
                            ins = t.transpose(trp[gb % 4][:, kk * 128:(kk + 1) * 128],
                                              xn_bf[:, j * 128:(j + 1) * 128],
                                              ident_bf[:])
                        ins.then_inc(sem["sAtb"], 1)

                def tblock(c):
                    t.wait_ge(sem["s_vd"], c + 1)
                    if c == 0:
                        t.wait_ge(sem["sAev"], 32)
                    for b in range(4):
                        n = 2 * c + b // 2
                        if b % 2 == 0:
                            if n >= 2:
                                t.wait_ge(sem["s_evA"], n - 1)
                        else:
                            if n >= 2:
                                t.wait_ge(sem["s_evB"], n - 1)
                        j0 = b * 8
                        for kk in range(8):
                            ins = t.transpose(trp[b % 4][:, kk * 128:(kk + 1) * 128],
                                              w1g_nat[:, (j0 + kk) * 128:(j0 + kk + 1) * 128],
                                              ident_bf[:])
                        ins.then_inc(sem["s_tb"], 1)

                def mmblock(c):
                    t.wait_ge(sem["s_evA"], 2 * (c + 1))
                    t.wait_ge(sem["s_evB"], 2 * (c + 1))
                    t.wait_ge(par("s_tu", c), 16 * (c // 2 + 1))
                    if c >= 1:
                        t.wait_ge(sem["s_ssil"], c)
                        t.wait_ge(sem["s_vh"], c)
                    for (ps, W, k) in ((psg, W1Tg, 0), (psu, W1Tu, 1)):
                        for j in range(FJ):
                            for th in range(2):
                                ins = t.matmul(
                                    ps[:, th * 512:(th + 1) * 512],
                                    W[c % 2][:, j * 128:(j + 1) * 128],
                                    xnT3[:, j, th * 512:(th + 1) * 512],
                                    start=(j == 0), stop=(j == FJ - 1))
                        ins.then_inc(sem["s_mm"], 1)

                for c in range(NC):
                    tblock(c)
                    if c >= 1:
                        mmblock(c - 1)
                mmblock(NC - 1)

        with contextlib.ExitStack() as es2:
            en2 = es2.enter_context
            h_grp = _multi(en2, nc, "sbuf", "h_grp", 2, [128, GS * T], BF16)
            qd_buf = _multi(en2, nc, "sbuf", "qd_buf", 2, [128, 2048], BF16)
            w1d_nat = en2(nc.sbuf_tensor("w1d_nat", [128, 2048], BF16))
            W1dT = _multi(en2, nc, "sbuf", "W1dT", 2, [128, GS * 2048], BF16)
            sdz = _multi(en2, nc, "sbuf", "sdz", 2, [128, 2 * 8 * 16], F32)
            dacc = en2(nc.sbuf_tensor("dacc", [128, 8 * 2048], F32))
            pb4 = _multi(en2, nc, "sbuf", "pb4", 2, [128, 2048], BF16)
            xs_buf = _multi(en2, nc, "sbuf", "xs_buf", 2, [128, 512], F32)
            xp_buf = _multi(en2, nc, "sbuf", "xp_buf", 2, [128, 512], F32)
            g_sb = _multi(en2, nc, "sbuf", "g_sb", 2, [128, 512], BF16)
            cAp = _multi(en2, nc, "sbuf", "cAp", 2, [128, 512], F32)
            cBp = _multi(en2, nc, "sbuf", "cBp", 2, [128, 512], F32)
            cC = en2(nc.psum_tensor("cC", [128, 512], F32))
            out_sb = _multi(en2, nc, "sbuf", "out_sb", 2, [128, 512], F32)
            punit = _multi(en2, nc, "psum", "punit", 4, [128, 512], F32)
            trp2 = _multi(en2, nc, "psum", "trp2", 2, [128, 1024], BF16)
            pps = en2(nc.psum_tensor("pps", [128, 128], F32))
            blk2 = en2(nc.Block())
            NSP_A = 16 * ((NC + 1) // 2)
            NSP_B = 16 * (NC // 2)

            def hw_barrier(eng):
                eng.wait_ge(sem["s_hwA"], NSP_A)
                eng.wait_ge(sem["s_hwB"], NSP_B)

            K5_OF = {}

            def xr_idx(idx):
                hh2, fc, tt = idx // 32, (idx % 32) // 8, idx % 8
                pt = (tt + 4) % 8
                col = hh2 * 2048 + fc * 512
                return hh2, fc, tt, pt, col

            @blk2.sync
            def _(sy):
                hw_barrier(sy)
                for (gg, hh, g, chs) in GG_LIST:
                    if gg >= 2:
                        sy.wait_ge(sem["s_mwd"], NW * (gg - 1))
                    src = h_sp[chs[0]:chs[0] + len(chs)].rearrange("c p t -> p c t")
                    dst = h_grp[gg % 2][:, 0:len(chs) * T]
                    dst = dst.rearrange("p (c t) -> p c t", t=T)
                    sy.dma_start(out=dst, in_=src).then_inc(par("s_hg", gg), 16)

            @blk2.gpsimd
            def _(g):
                hw_barrier(g)

                def sdz_grp(gg):
                    hh, gl = gg // NG, gg % NG
                    chs = _group_chunks(gl)
                    if gg >= 2:
                        g.wait_ge(sem["s_vdd"], PROD_LAST_K[gg - 2] + 1)
                    for ti, ten in enumerate([sd_r, zd_r]):
                        src = ten[chs[0]:chs[0] + len(chs), :,
                                  16 * hh:16 * hh + 16].rearrange("c p g -> p c g")
                        dst = sdz[gg % 2][:, ti * 128:ti * 128 + len(chs) * 16]
                        dst = dst.rearrange("p (c g) -> p c g", g=16)
                        g.dma_start(out=dst, in_=src).then_inc(par("s_sdd", gg), 16)

                def qd_dma(k):
                    gg, c = PROD[k]
                    hh = gg // NG
                    if k >= 2:
                        g.wait_ge(sem["s_vdd"], k - 1)
                    src = wd_r[c, :, 16 * hh:16 * hh + 16, :]
                    dst = qd_buf[k % 2].rearrange("p (a b) -> p a b", b=128)
                    g.dma_start(out=dst, in_=src,
                                max_dma_last_dim=8192).then_inc(par("s_qd", k), 16)

                def deq_d_g(k):
                    gg2, c2 = PROD[k]
                    hh2 = gg2 // NG
                    g.wait_ge(par("s_qd", k), 16 * (k // 2 + 1))
                    g.wait_ge(par("s_sdd", gg2), 32 * (gg2 // 2 + 1))
                    g.wait_ge(sem["s_tbd"], 2 * k)
                    cl2 = c2 % 8
                    for j in range(16):
                        jj = 16 * hh2 + j
                        ins = g.tensor_scalar(
                            w1d_nat[:, j * 128:(j + 1) * 128],
                            qd_buf[k % 2][:, j * 128:(j + 1) * 128],
                            sdz[gg2 % 2][:, 256 + cl2 * 32 + jj:256 + cl2 * 32 + jj + 1],
                            sdz[gg2 % 2][:, cl2 * 32 + jj:cl2 * 32 + jj + 1],
                            ALU.subtract, ALU.mult)
                    ins.then_inc(sem["s_vdd"], 1)

                sdz_grp(0)
                for k in range(PROD_FIRST_K[0], PROD_LAST_K[0] + 1):
                    qd_dma(k)
                for (gg, hh, gl, chs) in GG_LIST:
                    if gg + 1 < len(GG_LIST):
                        sdz_grp(gg + 1)
                        kf = PROD_FIRST_K[gg + 1]
                        kl = PROD_LAST_K[gg + 1]
                        for k in range(kf, kl + 1):
                            qd_dma(k)
                    if gg == 9 or gg == 21:
                        hh2 = 0 if gg == 9 else 1
                        if hh2 == 0:
                            g.wait_ge(sem["s_pd"], 16)
                        else:
                            g.wait_ge(sem["s_cmb"], 32)
                        for bi, ch in enumerate((0, 2)):
                            src = p_dram[ch:ch + 1, hh2 * 2048:(hh2 + 1) * 2048]
                            src = src.partition_broadcast(128)[:, 0, :]
                            g.dma_start(out=pb4[bi], in_=src,
                                        max_dma_last_dim=8192
                                        ).then_inc(sem["s_pb"], 16)
                g.wait_ge(sem["s_outA"], 16 * 32)
                g.wait_ge(sem["s_outB"], 16 * 32)

            @blk2.vector
            def _(v):
                hw_barrier(v)
                v.tensor_tensor(negrm[:], m_sb[:], r_sb[:], ALU.mult
                                ).then_inc(sem["s_cch"], 1)
                ccn0_rm = 1
                v.wait_ge(sem["s_cch"], ccn0_rm)
                v.tensor_scalar(negrm[:], negrm[:], -1.0, None, ALU.mult
                                ).then_inc(sem["s_rm"], 1)

                def p_chain():
                    v.wait_ge(sem["spg"], 16)
                    dA = dsum[:, 0:96:3]
                    dP = dsum[:, 1:96:3]
                    dB = dsum[:, 2:96:3]
                    v.tensor_tensor(p_q[:, 0:32], dA, dB, ALU.add
                                    ).then_inc(sem["s_pch"], 1)
                    v.wait_ge(sem["s_pch"], 1)
                    v.tensor_tensor(p_q[:, 32:64], p_q[:, 0:32], dP, ALU.subtract
                                    ).then_inc(sem["s_pch"], 1)
                    v.wait_ge(sem["s_pch"], 2)
                    v.scalar_tensor_tensor(p_q[:, 64:96], p_q[:, 32:64], 0.5, dA,
                                           ALU.mult, ALU.add).then_inc(sem["s_pch"], 1)
                    v.scalar_tensor_tensor(p_q[:, 96:128], p_q[:, 32:64], 0.5, dB,
                                           ALU.mult, ALU.add).then_inc(sem["s_pv1"], 1)

                def deq_d(k):
                    gg, c = PROD[k]
                    hh = gg // NG
                    v.wait_ge(par("s_qd", k), 16 * (k // 2 + 1))
                    v.wait_ge(par("s_sdd", gg), 32 * (gg // 2 + 1))
                    if k >= 1:
                        v.wait_ge(sem["s_tbd"], 2 * k)
                    cl = c % 8
                    for j in range(16):
                        ins = v.tensor_scalar(
                            w1d_nat[:, j * 128:(j + 1) * 128],
                            qd_buf[k % 2][:, j * 128:(j + 1) * 128],
                            sdz[gg % 2][:, 128 + cl * 16 + j:128 + cl * 16 + j + 1],
                            sdz[gg % 2][:, cl * 16 + j:cl * 16 + j + 1],
                            ALU.subtract, ALU.mult)
                    ins.then_inc(sem["s_vdd"], 1)

                def flush(gg, w):
                    W = NW * gg + w
                    v.wait_ge(sem["s_mwd"], W + 1)
                    first = (gg % NG == 0)
                    if first and gg >= NG:
                        v.wait_ge(sem["s_gs"], 2 * w + 2)
                    for u in range(2):
                        tt, fc = UNITS[2 * w + u]
                        dst = dacc[:, tt * 2048 + fc * 512:tt * 2048 + fc * 512 + 512]
                        if first:
                            ins = v.tensor_copy(dst, punit[(w % 2) * 2 + u])
                        else:
                            ins = v.tensor_tensor(dst, dst, punit[(w % 2) * 2 + u],
                                                  ALU.add)
                    ins.then_inc(sem["s_fld"], 1)

                ccn = [1]

                def cstep(ins):
                    ins.then_inc(sem["s_cch"], 1)
                    ccn[0] += 1
                    return ccn[0]

                def cmb(idx):
                    hh, fc, tt, pt, col = xr_idx(idx)
                    v.wait_ge(sem["s_pb"], 32 * (hh + 1))
                    v.wait_ge(par("s_xr", idx), 32 * (idx // 2 + 1))
                    v.wait_ge(sem["s_gs"], idx + 1)
                    v.wait_ge(sem["s_cab"], 2 * (idx + 1))
                    if idx >= 2:
                        v.wait_ge(par("s_out", idx), 16 * (idx // 2))
                    fs = slice(fc * 512, (fc + 1) * 512)
                    pbX = pb4[0][:, fs] if tt < 4 else pb4[1][:, fs]
                    cA_, cB_ = cAp[idx % 2], cBp[idx % 2]
                    first, second = (cA_, cB_) if tt < 4 else (cB_, cA_)
                    k3 = cstep(v.tensor_tensor(cC[:], first, second,
                                               ALU.subtract))
                    v.wait_ge(sem["s_cch"], k3)
                    k4 = cstep(v.tensor_tensor(first, cC[:], pbX, ALU.mult))
                    v.wait_ge(sem["s_cch"], k4)
                    k5 = cstep(v.scalar_tensor_tensor(cC[:], second, 2.0,
                                                      first, ALU.mult, ALU.add))
                    K5_OF[idx] = k5
                    v.wait_ge(sem["s_cch"], k5)
                    k6 = cstep(v.tensor_tensor(first, cC[:], g_sb[idx % 2],
                                               ALU.mult))
                    v.wait_ge(sem["s_cch"], k6)
                    v.tensor_tensor(out_sb[idx % 2], first, xs_buf[idx % 2],
                                    ALU.add).then_inc(sem["s_cmb"], 1)

                for k in range(PROD_FIRST_K[0], PROD_LAST_K[0] + 1):
                    deq_d(k)
                for (gg, hh, gl, chs) in GG_LIST:
                    nxt = (list(range(PROD_FIRST_K[gg + 1], PROD_LAST_K[gg + 1] + 1))
                           if gg + 1 < len(GG_LIST) else [])
                    for w in range(NW):
                        if w < len(nxt) and nxt[w] not in GDEQ:
                            deq_d(nxt[w])
                        if gg == NG and w >= 8:
                            cmb(2 * w)
                            cmb(2 * w + 1)
                        flush(gg, w)
                        if gg == 21 and w >= 1:
                            cmb(32 + 2 * (w - 1))
                            cmb(32 + 2 * (w - 1) + 1)
                    for k in nxt[NW:]:
                        if k not in GDEQ:
                            deq_d(k)
                    if gg == 7:
                        p_chain()
                    for idx in cmb_idxs_at(gg):
                        cmb(idx)
                for idx in range(62, 64):
                    cmb(idx)

            @blk2.scalar
            def _(s):
                hw_barrier(s)

                def p_sig():
                    s.wait_ge(sem["s_pv1"], 1)
                    s.activation(p_s[:, 0:32], p_q[:, 64:96],
                                 AF.Sigmoid, scale=1.0 / 64)
                    s.activation(p_s[:, 32:64], p_q[:, 64:96],
                                 AF.Sigmoid, scale=-1.0 / 64)
                    s.activation(p_s[:, 64:96], p_q[:, 96:128],
                                 AF.Sigmoid, scale=-1.0 / 64)
                    s.activation(p_s[:, 96:128], p_q[:, 96:128],
                                 AF.Sigmoid, scale=1.0 / 64
                                 ).then_inc(sem["s_sch"], 1)
                    s.wait_ge(sem["s_sch"], 1)
                    s.add(p_s[:, 0:32], p_s[:, 0:32], 1.0)
                    s.add(p_s[:, 96:128], p_s[:, 96:128], 1.0
                          ).then_inc(sem["s_ps1"], 1)

                def p_store():
                    s.wait_ge(sem["s_ptr"], 1)
                    s.copy(p_qT[:], pps[:]).then_inc(sem["s_sch"], 1)
                    s.wait_ge(sem["s_sch"], 2)
                    pd = p_dram.rearrange("c (j q) -> (c j) q", q=128)
                    s.dma_start(out=pd, in_=p_qT[:]).then_inc(sem["s_pd"], 16)

                def evac_dA(k):
                    gg, c = PROD[k]
                    slot = c % 8
                    s.wait_ge(sem["s_tbd"], 2 * k + 1)
                    if gg >= 2:
                        s.wait_ge(sem["s_mwd"], NW * (gg - 1))
                    dst = W1dT[gg % 2][:, slot * 2048:slot * 2048 + 1024]
                    s.copy(dst, trp2[0]).then_inc(sem["s_evdA"], 1)

                def evac_dB(k):
                    gg, c = PROD[k]
                    slot = c % 8
                    s.wait_ge(sem["s_tbd"], 2 * k + 2)
                    if gg >= 2:
                        s.wait_ge(sem["s_mwd"], NW * (gg - 1))
                    dst = W1dT[gg % 2][:, slot * 2048 + 1024:slot * 2048 + 2048]
                    s.copy(dst, trp2[1]).then_inc(sem["s_evdB"], 1)

                def xr_pair(idx):
                    hh2, fc, tt, pt, col = xr_idx(idx)
                    s.dma_start(
                        out=xs_buf[idx % 2],
                        in_=x_rows[tt * 128:(tt + 1) * 128, col:col + 512]
                    ).then_inc(par("s_xr", idx), 16)
                    s.dma_start(
                        out=xp_buf[idx % 2],
                        in_=x_rows[pt * 128:(pt + 1) * 128, col:col + 512]
                    ).then_inc(par("s_xr", idx), 16)

                def sg_part1(idx):
                    hh, fc, tt, pt, col = xr_idx(idx)
                    ui = fc * 8 + tt
                    gg_last = NG - 1 + hh * NG
                    s.wait_ge(sem["s_fld"], NW * gg_last + ui // 2 + 1)
                    if idx >= 2:
                        s.wait_ge(sem["s_cmb"], idx - 1)
                    src = dacc[:, tt * 2048 + fc * 512:tt * 2048 + fc * 512 + 512]
                    s.activation(g_sb[idx % 2], src, AF.Silu
                                 ).then_inc(sem["s_gs"], 1)
                    if idx == 0:
                        s.wait_ge(sem["s_rm"], 1)
                    s.wait_ge(par("s_xr", idx), 32 * (idx // 2 + 1))
                    if idx >= 2:
                        s.wait_ge(sem["s_cch"], K5_OF[idx - 2])
                    s.activation(cAp[idx % 2], xs_buf[idx % 2], AF.Identity,
                                 scale=r_sb[:, tt:tt + 1],
                                 bias=negrm[:, tt:tt + 1]).then_inc(sem["s_cab"], 1)
                    s.activation(cBp[idx % 2], xp_buf[idx % 2], AF.Identity,
                                 scale=r_sb[:, pt:pt + 1],
                                 bias=negrm[:, pt:pt + 1]).then_inc(sem["s_cab"], 1)

                def sg_part2(idx):
                    hh, fc, tt, pt, col = xr_idx(idx)
                    s.wait_ge(sem["s_cmb"], idx + 1)
                    s.dma_start(out=out_rows[tt * 128:(tt + 1) * 128,
                                             col:col + 512],
                                in_=out_sb[idx % 2]).then_inc(par("s_out", idx), 16)
                    if idx + 2 < 64:
                        xr_pair(idx + 2)

                def sg_out(idx):
                    if idx >= 2:
                        sg_part2(idx - 2)
                    sg_part1(idx)
                    if idx == 63:
                        sg_part2(62)
                        sg_part2(63)

                for k in range(PROD_FIRST_K[0], PROD_LAST_K[0] + 1):
                    evac_dA(k)
                    evac_dB(k)
                for (gg, hh, gl, chs) in GG_LIST:
                    if gg + 1 < len(GG_LIST):
                        for k in range(PROD_FIRST_K[gg + 1], PROD_LAST_K[gg + 1] + 1):
                            evac_dA(k)
                            evac_dB(k)
                    if gg == 8:
                        p_sig()
                    if gg == 9:
                        p_store()
                        xr_pair(0)
                        xr_pair(1)
                    for idx in cmb_idxs_at(gg):
                        sg_out(idx)
                    if gg == NG:
                        for idx in range(16, 32):
                            sg_out(idx)
                for idx in range(32, 64):
                    sg_out(idx)

            @blk2.tensor
            def _(t):
                hw_barrier(t)

                def tb_d(k):
                    gg, c = PROD[k]
                    t.wait_ge(sem["s_vdd"], k + 1)
                    for b in range(2):
                        if k >= 1:
                            t.wait_ge(sem["s_evdA" if b == 0 else "s_evdB"], k)
                        for kk in range(8):
                            j = b * 8 + kk
                            ins = t.transpose(trp2[b][:, kk * 128:(kk + 1) * 128],
                                              w1d_nat[:, j * 128:(j + 1) * 128],
                                              ident_bf[:])
                        ins.then_inc(sem["s_tbd"], 1)

                def wave(gg, gl, chs, w):
                    W = NW * gg + w
                    if W >= 2:
                        t.wait_ge(sem["s_fld"], W - 1)
                    if w == 0:
                        t.wait_ge(par("s_hg", gg), 16 * (gg // 2 + 1))
                    k0 = PROD_FIRST_K[gg]
                    for ci in range(len(chs)):
                        if w == 0:
                            t.wait_ge(sem["s_evdA"], k0 + ci + 1)
                            t.wait_ge(sem["s_evdB"], k0 + ci + 1)
                        for u in range(2):
                            tt, fc = UNITS[2 * w + u]
                            ins = t.matmul(
                                punit[(w % 2) * 2 + u],
                                h_grp[gg % 2][:, ci * T + tt * 128:ci * T + tt * 128 + 128],
                                W1dT[gg % 2][:, ci * 2048 + fc * 512:ci * 2048 + fc * 512 + 512],
                                start=(ci == 0), stop=(ci == len(chs) - 1))
                    ins.then_inc(sem["s_mwd"], 1)

                for k in range(PROD_FIRST_K[0], PROD_LAST_K[0] + 1):
                    tb_d(k)
                for (gg, hh, gl, chs) in GG_LIST:
                    nxt = (list(range(PROD_FIRST_K[gg + 1], PROD_LAST_K[gg + 1] + 1))
                           if gg + 1 < len(GG_LIST) else [])
                    for w in range(NW):
                        wave(gg, gl, chs, w)
                        if w < len(nxt):
                            tb_d(nxt[w])
                    for k in nxt[NW:]:
                        tb_d(k)
                    if gg == 8:
                        t.wait_ge(sem["s_ps1"], 1)
                        t.transpose(pps[:], p_s[:], ident_f[:]).then_inc(sem["s_ptr"], 1)

    return nc


def kernel(L0_attn_x, L0_ffn_gate_qweight, L0_ffn_gate_scales, L0_ffn_gate_zeros,
           L0_ffn_up_qweight, L0_ffn_up_scales, L0_ffn_up_zeros,
           L0_ffn_down_qweight, L0_ffn_down_scales, L0_ffn_down_zeros):
    x = np.ascontiguousarray(np.asarray(L0_attn_x, dtype=np.float32))
    qg = np.ascontiguousarray(np.asarray(L0_ffn_gate_qweight, dtype=np.int32))
    qu = np.ascontiguousarray(np.asarray(L0_ffn_up_qweight, dtype=np.int32))
    qd = np.asarray(L0_ffn_down_qweight, dtype=np.int32)
    wd_r = np.ascontiguousarray(
        qd.reshape(FJ, 128, NC, 128).transpose(2, 1, 0, 3))
    sd = np.asarray(L0_ffn_down_scales, dtype=np.float32)
    zd = np.asarray(L0_ffn_down_zeros, dtype=np.float32)
    sd_r = np.ascontiguousarray(sd.reshape(FJ, 128, NC).transpose(2, 1, 0))
    zd_r = np.ascontiguousarray(zd.reshape(FJ, 128, NC).transpose(2, 1, 0))
    shared = {
        "qg": qg, "qu": qu, "wd_r": wd_r,
        "sg": np.ascontiguousarray(np.asarray(L0_ffn_gate_scales, np.float32)),
        "zg": np.ascontiguousarray(np.asarray(L0_ffn_gate_zeros, np.float32)),
        "su": np.ascontiguousarray(np.asarray(L0_ffn_up_scales, np.float32)),
        "zu": np.ascontiguousarray(np.asarray(L0_ffn_up_zeros, np.float32)),
        "sd_r": sd_r, "zd_r": zd_r,
        "ident": np.eye(128, dtype=np.float32),
    }
    nc = build_nc()
    in_maps = []
    for k in range(NCORES):
        m = dict(shared)
        m["x_rows"] = np.ascontiguousarray(
            x[:, k * 512:(k + 1) * 512, :]).reshape(T, F)
        in_maps.append(m)
    res = run_bass_kernel_spmd(nc, in_maps, core_ids=list(range(NCORES)))
    out = np.empty((2, 4096, 4096), np.float32)
    for k in range(NCORES):
        out[:, k * 512:(k + 1) * 512, :] = res.results[k]["out_rows"].reshape(2, 512, F)
    return out
